# version 13
# baseline (speedup 1.0000x reference)
import numpy as np
import ml_dtypes
import concourse.bass as bass
import concourse.mybir as mybir

F32 = mybir.dt.float32
BF16 = mybir.dt.bfloat16
AF = mybir.ActivationFunctionType
ALU = mybir.AluOpType
AX = mybir.AxisListType

ENGINES = ("sync", "scalar", "vector", "tensor", "gpsimd")
D = 1024
T = 2048
CT = 512
NCH = T // CT
FC = D // 128
GAMMA = 0.96875
PAIRS = [[0, 1], [2, 3], [4, 5], [6, 7]]


class Buf:
    __slots__ = ("name", "w", "r")
    def __init__(self, name):
        self.name, self.w, self.r = name, None, []


class Op:
    __slots__ = ("idx", "eng", "fn", "deps", "kind", "inc", "sem", "cum_after",
                 "needs_inc", "cc_sem", "key")
    def __init__(self, idx, eng, fn, kind):
        self.idx, self.eng, self.fn, self.kind = idx, eng, fn, kind
        self.deps = set()
        self.cc_sem = None
        self.key = None


class Flow:
    TOTAL_KEYS = ("init",)

    def __init__(self):
        self.ops = []
        self.per_eng = {e: [] for e in ENGINES}
        self.n_cc = 0

    def emit(self, eng, fn, reads=(), writes=(), deps=(), kind="op", key=None):
        op = Op(len(self.ops), eng, fn, kind)
        op.key = key
        for d in deps:
            if d is not None:
                op.deps.add(d)
        for b in reads:
            if b.w is not None:
                op.deps.add(b.w)
        for b in writes:
            if b.w is not None:
                op.deps.add(b.w)
            for rop in b.r:
                op.deps.add(rop)
        for b in reads:
            b.r.append(op)
        for b in writes:
            b.w = op
            b.r = []
        if kind == "cc":
            op.cc_sem = self.n_cc
            self.n_cc += 1
        self.ops.append(op)
        self.per_eng[eng].append(op)
        return op

    def lower(self, nc, stack):
        eng_sem = {e: stack.enter_context(nc.semaphore(f"es_{e}")) for e in ENGINES}
        keys = []
        for op in self.ops:
            if op.kind in ("dma", "switch"):
                assert op.key is not None, f"dma op {op.idx} missing sem key"
                if op.key not in keys:
                    keys.append(op.key)
        key_sem = {k: stack.enter_context(nc.semaphore(f"k_{k}")) for k in keys}
        cc_sems = [stack.enter_context(nc.semaphore(f"cc_{i}")) for i in range(self.n_cc)]
        used = set()
        for op in self.ops:
            for d in op.deps:
                used.add(d)
        cum = {}
        for op in self.ops:
            if op.kind in ("dma", "switch"):
                op.sem, op.inc = key_sem[op.key], 16
                op.needs_inc = True
            elif op.kind == "cc":
                op.sem, op.inc = cc_sems[op.cc_sem], 1
                op.needs_inc = True
            else:
                op.sem, op.inc = eng_sem[op.eng], 1
                op.needs_inc = op in used
            if op.needs_inc:
                k = op.sem.name
                cum[k] = cum.get(k, 0) + op.inc
                op.cum_after = cum[k]
            else:
                op.cum_after = None
        for op in self.ops:
            if op.kind in ("dma", "switch") and op.key.endswith("_init"):
                op.cum_after = cum[op.sem.name]
        totals = dict(cum)

        block = stack.enter_context(nc.Block())

        def make_runner(e):
            def run(eng):
                waited = {}
                for op in self.per_eng[e]:
                    wants = {}
                    for d in op.deps:
                        assert d.needs_inc and d.cum_after is not None, \
                            f"dep {d.idx} of {op.idx} not inc'd"
                        k = d.sem.name
                        wants[k] = max(wants.get(k, (0, None))[0], d.cum_after), d.sem
                    for k, (val, sem) in wants.items():
                        if waited.get(k, 0) < val:
                            eng.wait_ge(sem, val)
                            waited[k] = val
                    inc_args = (op.sem, op.inc) if op.needs_inc else None
                    if op.kind == "switch":
                        op.fn(eng, inc_args)
                    else:
                        fns = op.fn if isinstance(op.fn, (list, tuple)) else (op.fn,)
                        inst = None
                        for f in fns:
                            inst = f(eng)
                        if op.needs_inc:
                            inst.then_inc(op.sem, op.inc)
            return run

        for e in ENGINES:
            getattr(block, e)(make_runner(e))
        return eng_sem, key_sem, cc_sems, totals


def pack_w(WT):
    WT = np.asarray(WT)
    K, M = WT.shape
    H = WT.reshape(K // 128, 128, M // 128, 128).transpose(2, 1, 0, 3)
    return np.ascontiguousarray(H.reshape(M // 128, 128, K)).astype(np.float32)


def pack_w_mov(WT):
    WT = np.asarray(WT)
    K, M = WT.shape
    return np.ascontiguousarray(WT.reshape(K // 128, 128, M)).astype(np.float32)


def pack_bias(b):
    return np.ascontiguousarray(np.asarray(b).reshape(-1, 128).T).astype(np.float32)


class View:
    __slots__ = ("_ap",)
    def __init__(self, ap):
        self._ap = ap
    def ap(self):
        return self._ap
    def __getitem__(self, k):
        return self._ap[k]


class MK:
    def __init__(self, dbg_names=()):
        self.nc = bass.Bass("TRN2", target_bir_lowering=False, debug=False)
        self.f = Flow()
        self.inputs = {}
        self.outputs = {}
        self.t = {}
        self.bufs = {}
        self.dbg_names = set(dbg_names)
        self._dbgs = {}
        self._exit = None
        self.all_bufs = []

    def din(self, name, shape, dtype=F32):
        self.inputs[name] = (tuple(shape), dtype)
        return self.nc.dram_tensor(name, list(shape), dtype, kind="ExternalInput").ap()

    def dout(self, name, shape, dtype=F32):
        self.outputs[name] = (tuple(shape), dtype)
        return self.nc.dram_tensor(name, list(shape), dtype, kind="ExternalOutput").ap()

    def dram(self, name, shape, dtype=F32):
        return self.nc.dram_tensor(name, list(shape), dtype, kind="Internal").ap()

    def _reg(self, name, ap, buf):
        v = View(ap)
        self.t[name] = v
        if buf is None:
            buf = Buf(name)
            self.all_bufs.append(buf)
        self.bufs[name] = buf
        return v

    def sb(self, name, shape, dtype=F32):
        t = self.nc.alloc_sbuf_tensor(name, list(shape), dtype)
        return self._reg(name, t.ap(), None)

    def view(self, name, base, dtype, start, cols, share=None):
        bap = self.t[base].ap()
        ap = bap.bitcast(dtype)[:, start:start + cols]
        return self._reg(name, ap, self.bufs[share] if share else None)

    def ps(self, name, shape=(128, 512), dtype=F32):
        t = self._exit.enter_context(self.nc.psum_tensor(name, list(shape), dtype))
        return self._reg(name, t.ap(), None)

    def B(self, name):
        return self.bufs[name]

    def dma(self, eng, out_ap, in_ap, reads=(), writes=(), deps=(), key=None, slow=False):
        if key is None:
            key = writes[0].name if writes else "out"
        key = eng[:2] + "_" + key
        return self.f.emit(
            eng, lambda e, o=out_ap, i=in_ap, sl=slow: e.dma_start(
                out=o, in_=i, allow_slow_non_contiguous=sl),
            reads=reads, writes=writes, deps=deps, kind="dma", key=key)

    def op(self, eng, fn, reads=(), writes=(), deps=()):
        return self.f.emit(eng, fn, reads=reads, writes=writes, deps=deps, kind="op")

    def cc(self, fn, reads=(), writes=(), deps=()):
        return self.f.emit("gpsimd", fn, reads=reads, writes=writes, deps=deps, kind="cc")

    def switch(self, eng, fn, reads=(), writes=(), deps=(), key=None):
        return self.f.emit(eng, fn, reads=reads, writes=writes, deps=deps,
                           kind="switch", key=key or "sw")

    def barrier(self):
        deps = set()
        for b in self.all_bufs:
            if b.w is not None:
                deps.add(b.w)
            deps.update(b.r)
        bop = self.f.emit("vector", lambda e: e.engine_nop(), deps=tuple(deps), kind="op")
        for b in self.all_bufs:
            b.w = bop
            b.r = []
        return bop

    def dump_stream(self, name, src_ap, cols, c, dtype=F32, reads=(), n=NCH):
        if name not in self.dbg_names:
            return
        key = "dbgs_" + name
        if key not in self._dbgs:
            self._dbgs[key] = self.dout(key, [128, cols * n], dtype)
        self.dma("sync", self._dbgs[key][:, c * cols:(c + 1) * cols], src_ap, reads=reads,
                 key=f"dmp_{name}{c}")


def layernorm(mk, src, c, dst, src_seg=None, src_bufs=None, dst_sl=None,
              patches=(), affine=None, u_name="ln_u"):
    xb, sq = mk.t["ln_xb"], mk.t["ln_sq"]
    m_rep, msq, r_rep, u = mk.t["ln_m"], mk.t["ln_msq"], mk.t["ln_r"], mk.t[u_name]
    ones_bf, eps_t = mk.t["ones_bf"], mk.t["eps"]
    psA, psB = mk.t["psA"], mk.t["psB"]
    t0 = c * CT
    if src_seg is None:
        x_t = mk.t[src]
        src_seg = lambda fc: [(0, CT, x_t[:, fc * T + t0: fc * T + t0 + CT])]
        src_bufs = (mk.B(src),)
    if dst_sl is None:
        xt_ = mk.t[dst]
        dst_sl = lambda fc: (xt_[:, fc * CT:(fc + 1) * CT], mk.B(dst))
    pat_bufs = tuple(b for _, _, b in patches)
    for fc in range(FC):
        fns = []
        for off, wdt, sap in src_seg(fc):
            fns.append(lambda e, s=sap, d=xb[:, fc * CT + off: fc * CT + off + wdt]:
                       e.tensor_copy(d, s))
        for pfc, cap, _b in patches:
            if pfc == fc:
                fns.append(lambda e, s=cap, d=xb[:, fc * CT: fc * CT + 1]:
                           e.tensor_copy(d, s))
        mk.op("vector", fns, reads=tuple(src_bufs) + pat_bufs, writes=(mk.B("ln_xb"),))
        mk.op("scalar", lambda e, k=fc: e.activation(
            out=sq[:, k * CT:(k + 1) * CT], in_=xb[:, k * CT:(k + 1) * CT],
            func=AF.Square),
            reads=(mk.B("ln_xb"),), writes=(mk.B("ln_sq"),))
        mk.op("tensor", lambda e, k=fc: e.matmul(
            psA[:, :], ones_bf[:, :], xb[:, k * CT:(k + 1) * CT],
            start=(k == 0), stop=(k == FC - 1)),
            reads=(mk.B("ln_xb"), mk.B("ones_bf")), writes=(mk.B("psA"),))
        mk.op("tensor", lambda e, k=fc: e.matmul(
            psB[:, :], ones_bf[:, :], sq[:, k * CT:(k + 1) * CT],
            start=(k == 0), stop=(k == FC - 1)),
            reads=(mk.B("ln_sq"), mk.B("ones_bf")), writes=(mk.B("psB"),))
    mk.op("vector", lambda e: e.tensor_scalar_mul(m_rep[:, :], psA[:, :], 1.0 / D),
          reads=(mk.B("psA"),), writes=(mk.B("ln_m"),))
    mk.op("vector", lambda e: e.tensor_mul(msq[:, :], m_rep[:, :], m_rep[:, :]),
          reads=(mk.B("ln_m"),), writes=(mk.B("ln_msq"),))
    mk.op("vector", lambda e: e.scalar_tensor_tensor(msq[:, :], psB[:, :], 1.0 / D,
                                                     msq[:, :], ALU.mult, ALU.subtract),
          reads=(mk.B("psB"), mk.B("ln_msq")), writes=(mk.B("ln_msq"),))
    mk.op("scalar", lambda e: e.activation(out=r_rep[:, :], in_=msq[:, :], func=AF.Ln,
                                           bias=eps_t[:, 0:1]),
          reads=(mk.B("ln_msq"), mk.B("eps")), writes=(mk.B("ln_r"),))
    mk.op("scalar", lambda e: e.activation(out=r_rep[:, :], in_=r_rep[:, :], func=AF.Exp,
                                           scale=-0.5),
          reads=(mk.B("ln_r"),), writes=(mk.B("ln_r"),))
    fns = []
    for fc in range(FC):
        for off, wdt, sap in src_seg(fc):
            fns.append(lambda e, s=sap, d=u[:, fc * CT + off: fc * CT + off + wdt],
                       o=off, w_=wdt: e.tensor_sub(d, s, m_rep[:, o:o + w_]))
    for fc, cap, _b in patches:
        fns.append(lambda e, s=cap, d=u[:, fc * CT: fc * CT + 1]:
                   e.tensor_sub(d, s, m_rep[:, 0:1]))
    mk.op("vector", fns, reads=tuple(src_bufs) + pat_bufs + (mk.B("ln_m"),),
          writes=(mk.B(u_name),))
    fns = []
    dbufs = set()
    for fc in range(FC):
        dap, dbuf = dst_sl(fc)
        dbufs.add(dbuf)
        fns.append(lambda e, d=dap, k=fc: e.tensor_mul(
            d, u[:, k * CT:(k + 1) * CT], r_rep[:, :]))
    mk.op("vector", fns, reads=(mk.B(u_name), mk.B("ln_r")), writes=tuple(dbufs))
    if affine is not None:
        lg, lb, abufs = affine
        fns = []
        for fc in range(FC):
            dap, dbuf = dst_sl(fc)
            fns.append(lambda e, d=dap, k=fc: e.tensor_scalar(
                d, d, lg[:, k:k + 1], lb[:, k:k + 1], ALU.mult, ALU.add))
        mk.op("vector", fns, reads=tuple(dbufs) + tuple(abufs), writes=tuple(dbufs))


def proj(mk, w_hbm, src, out_cb, n_mc=FC, kc_n=FC, rhs_cols=CT, ld_eng="gpsimd",
         w_buf=None, ps_names=("psM0", "psM1"), w_names=None, w_rbuf=None):
    xt = mk.t[src]
    Bxt = mk.B(src)
    if w_names is None:
        w_names = ((w_buf or "w") + "0", (w_buf or "w") + "1")
    for mc in range(n_mc):
        wn = w_names[mc % len(w_names)]
        wb = mk.t[wn]
        Bwb = mk.B(wn)
        mk.dma(ld_eng, wb[:, 0:kc_n * 128], w_hbm[mc, :, :], writes=(Bwb,), key=wn,
               reads=(() if w_rbuf is None else (w_rbuf,)))
        psn = ps_names[mc % len(ps_names)]
        ps = mk.t[psn]
        Bps = mk.B(psn)
        mk.op("tensor",
              [(lambda e, k=kc, p=ps, w=wb: e.matmul(
                  p[:, 0:rhs_cols], w[:, k * 128:(k + 1) * 128],
                  xt[:, k * rhs_cols:(k + 1) * rhs_cols],
                  start=(k == 0), stop=(k == kc_n - 1))) for kc in range(kc_n)],
              reads=(Bwb, Bxt), writes=(Bps,))
        out_cb(mc, ps, Bps)


def pair_exchange(mk, in_dram_ap, out_dram_ap):
    return mk.cc(lambda e: e.collective_compute(
        "AllGather", ALU.bypass, replica_groups=PAIRS,
        ins=[in_dram_ap.tensor.ap().opt()], outs=[out_dram_ap.tensor.ap().opt()]))


def build_attn_a(mk, W):
    x_t, xt, xp = mk.t["x_fm"], mk.t["xt"], mk.t["xp"]
    Vb, Bb, hch = mk.t["Vb"], mk.t["Bb"], mk.t["hch"]
    hfull, pfull, tmpS = mk.t["hfull"], mk.t["pfull"], mk.t["tmpS"]
    hcar, pcar, hin, hinm = mk.t["hcar"], mk.t["pcar"], mk.t["hin"], mk.t["hinm"]
    zb, c0_t = mk.t["zb"], mk.t["c0_t"]
    bal, bbe, bvv = mk.t["bal"], mk.t["bbe"], mk.t["bvv"]
    Bx = mk.B("x_fm")

    mk.op("vector", lambda e: e.memset(pcar[:, :], 1.0), writes=(mk.B("pcar"),))
    mk.op("vector", lambda e: e.memset(hcar[:, :], 0.0), writes=(mk.B("hcar"),))

    for c in range(NCH):
        layernorm(mk, "x_fm", c, "xt")
        def ev_xp(mc, ps, Bps):
            mk.op("scalar", lambda e, m=mc, p=ps: e.activation(
                out=xp[:, m * CT:(m + 1) * CT], in_=p[:, 0:CT], func=AF.Silu),
                reads=(Bps,), writes=(mk.B("xp"),))
        proj(mk, mk.t["wst_ain"].ap(), "xt", ev_xp, ld_eng="sync",
             w_rbuf=mk.B("wst_ain"), ps_names=("psM0", "psM1", "psU", "psV"))
        def ev_v(mc, ps, Bps):
            mk.op("scalar", lambda e, m=mc, p=ps: e.activation(
                out=Vb[:, m * CT:(m + 1) * CT], in_=p[:, 0:CT], func=AF.Tanh,
                bias=bvv[:, m:m + 1]),
                reads=(Bps, mk.B("bvv")), writes=(mk.B("Vb"),))
        proj(mk, mk.t["wst_av"].ap(), "xp", ev_v, ld_eng="sync",
             w_rbuf=mk.B("wst_av"), ps_names=("psM0", "psM1", "psU", "psV"))
        def ev_b(mc, ps, Bps):
            mk.op("scalar", lambda e, m=mc, p=ps: e.activation(
                out=Bb[:, m * CT:(m + 1) * CT], in_=p[:, 0:CT], func=AF.Sigmoid,
                bias=bbe[:, m:m + 1]),
                reads=(Bps, mk.B("bbe")), writes=(mk.B("Bb"),))
        proj(mk, mk.t["wst_abe"].ap(), "xp", ev_b, ld_eng="sync",
             w_rbuf=mk.B("wst_abe"), ps_names=("psM0", "psM1", "psU", "psV"))
        mk.op("vector",
              [(lambda e, k=fc: e.tensor_mul(
                  Vb[:, k * CT:(k + 1) * CT], Vb[:, k * CT:(k + 1) * CT],
                  Bb[:, k * CT:(k + 1) * CT])) for fc in range(FC)],
              reads=(mk.B("Vb"), mk.B("Bb")), writes=(mk.B("Vb"),))
        def ev_a(mc, ps, Bps):
            mk.op("scalar", lambda e, m=mc, p=ps: e.activation(
                out=Bb[:, m * CT:(m + 1) * CT], in_=p[:, 0:CT], func=AF.Sigmoid,
                bias=bal[:, m:m + 1]),
                reads=(Bps, mk.B("bal")), writes=(mk.B("Bb"),))
        proj(mk, mk.t["wst_aal"].ap(), "xp", ev_a, ld_eng="sync",
             w_rbuf=mk.B("wst_aal"), ps_names=("psM0", "psM1", "psU", "psV"))
        mk.op("vector",
              [(lambda e, k=fc: e.tensor_tensor_scan(
                  hch[:, k * CT:(k + 1) * CT], Bb[:, k * CT:(k + 1) * CT],
                  Vb[:, k * CT:(k + 1) * CT], hcar[:, k:k + 1], ALU.mult, ALU.add))
               for fc in range(FC)],
              reads=(mk.B("Bb"), mk.B("Vb"), mk.B("hcar")), writes=(mk.B("hch"),))
        mk.op("vector", lambda e: e.tensor_copy(hcar[:, 0:FC], hch[:, CT - 1::CT]),
              reads=(mk.B("hch"),), writes=(mk.B("hcar"),))
        mk.op("scalar",
              [(lambda e, k=fc, cc_=c: e.activation(
                  out=hfull[:, k * T + cc_ * CT: k * T + cc_ * CT + CT],
                  in_=hch[:, k * CT:(k + 1) * CT], func=AF.Identity)) for fc in range(FC)],
              reads=(mk.B("hch"),), writes=(mk.B("hfull"),))
        mk.op("vector",
              [(lambda e, k=fc: e.tensor_tensor_scan(
                  Vb[:, k * CT:(k + 1) * CT], Bb[:, k * CT:(k + 1) * CT],
                  zb[:, 0:CT], pcar[:, k:k + 1], ALU.mult, ALU.add)) for fc in range(FC)],
              reads=(mk.B("Bb"), mk.B("zb"), mk.B("pcar")), writes=(mk.B("Vb"),))
        mk.op("vector", lambda e: e.tensor_copy(pcar[:, 0:FC], Vb[:, CT - 1::CT]),
              reads=(mk.B("Vb"),), writes=(mk.B("pcar"),))
        mk.op("scalar",
              [(lambda e, k=fc, cc_=c: e.activation(
                  out=pfull[:, k * T + cc_ * CT: k * T + cc_ * CT + CT],
                  in_=Vb[:, k * CT:(k + 1) * CT], func=AF.Identity)) for fc in range(FC)],
              reads=(mk.B("Vb"),), writes=(mk.B("pfull"),))

    x1_in, x1_out = mk.t["x1_in"].ap(), mk.t["x1_out"].ap()
    bounce1 = mk.dma("gpsimd", x1_in[:, :], hcar[:, 0:FC], reads=(mk.B("hcar"),), key="xch1")
    cc1 = pair_exchange(mk, x1_in, x1_out)
    cc1.deps.add(bounce1)

    def x1_consume(eng, inc_args):
        for arm in eng.switch_core_id(n=128):
            if arm.logical % 2 == 1:
                eng.wait_ge(cc1.sem, 1)
                inst = eng.dma_start(out=hin[:, :], in_=x1_out[0:128, :])
            else:
                inst = eng.dma_start(out=hin[:, :], in_=x1_in[:, :])
            if inc_args:
                inst.then_inc(*inc_args)
    mk.switch("gpsimd", x1_consume, writes=(mk.B("hin"),), deps=(bounce1,), key="sw1")
    mk.op("vector", lambda e: e.tensor_scalar_mul(hinm[:, :], hin[:, :], c0_t[:, 0:1]),
          reads=(mk.B("hin"), mk.B("c0_t")), writes=(mk.B("hinm"),))

    for c in range(NCH):
        mk.op("vector",
              [(lambda e, k=fc, s=slice(fc * T + c * CT, fc * T + c * CT + CT):
                e.scalar_tensor_tensor(
                    tmpS[:, k * CT:(k + 1) * CT], pfull[:, s], hinm[:, k:k + 1],
                    hfull[:, s], ALU.mult, ALU.add)) for fc in range(FC)],
              reads=(mk.B("pfull"), mk.B("hfull"), mk.B("hinm")), writes=(mk.B("tmpS"),))
        mk.op("scalar", lambda e: e.activation(out=hch[:, :], in_=tmpS[:, :], func=AF.Silu),
              reads=(mk.B("tmpS"),), writes=(mk.B("hch"),))
        mk.op("vector",
              [(lambda e, k=fc: e.tensor_mul(
                  xt[:, k * CT:(k + 1) * CT], tmpS[:, k * CT:(k + 1) * CT],
                  hch[:, k * CT:(k + 1) * CT])) for fc in range(FC)],
              reads=(mk.B("tmpS"), mk.B("hch")), writes=(mk.B("xt"),))
        def ev_o(mc, ps, Bps, cc_=c):
            s = slice(mc * T + cc_ * CT, mc * T + cc_ * CT + CT)
            mk.op("vector", lambda e, ss=s, p=ps: e.tensor_add(
                x_t[:, ss], x_t[:, ss], p[:, 0:CT]),
                reads=(Bps, Bx), writes=(Bx,))
        proj(mk, mk.t["wst_aout"].ap(), "xt", ev_o, ld_eng="sync",
             w_rbuf=mk.B("wst_aout"), ps_names=("psM0", "psM1", "psU", "psV"))


def build_attn_b(mk, W):
    x_t = mk.t["x_fm"]
    Bx = mk.B("x_fm")
    xt = mk.t["xt"]
    q_fm, k_fm, vtm, ktm = mk.t["q_fm"], mk.t["k_fm"], mk.t["vtm"], mk.t["ktm"]
    ptr, qgs, qgc, A_sb = mk.t["ptr"], mk.t["qgs"], mk.t["qgc"], mk.t["A_sb"]
    wvm, gk_t, gq_t, gc_t = mk.t["wvm"], mk.t["gk_t"], mk.t["gq_t"], mk.t["gc_t"]
    dmask = mk.t["dmask_t"]
    bbq, bbk, eye = mk.t["bbq_t"], mk.t["bbk_t"], mk.t["eye_t"]
    psT = mk.t["psT"]
    Ain = [mk.t["Ain0"], mk.t["Ain1"]]

    mk.dma("gpsimd", wvm.ap().rearrange("p (k m) -> p k m", k=FC),
           W["wb_v"].transpose([1, 0, 2]), writes=(mk.B("wvm"),), key="wvm")
    mk.dma("gpsimd", gq_t[:, :], W["gq"][:, :], writes=(mk.B("gq_t"),), key="gqd")
    mk.dma("gpsimd", gk_t[:, :], W["gk"][:, :], writes=(mk.B("gk_t"),), key="gkd")
    mk.dma("gpsimd", dmask[:, :], W["dm"][:, :], writes=(mk.B("dmask_t"),), key="dmd")
    mk.dma("gpsimd", gc_t[:, :], W["gc"][:, :], writes=(mk.B("gc_t"),), key="gcd")
    mk.op("vector", lambda e: e.memset(A_sb[:, :], 0.0), writes=(mk.B("A_sb"),))

    for c in range(NCH):
        layernorm(mk, "x_fm", c, "xt")
        def ev_q(mc, ps, Bps):
            mk.op("scalar", lambda e, m=mc, p=ps: e.activation(
                out=q_fm[:, m * CT:(m + 1) * CT], in_=p[:, 0:CT], func=AF.Identity,
                bias=bbq[:, m:m + 1]),
                reads=(Bps, mk.B("bbq_t")), writes=(mk.B("q_fm"),))
        proj(mk, mk.t["wst_bq"].ap(), "xt", ev_q, ld_eng="sync",
             w_rbuf=mk.B("wst_bq"), w_names=("w0", "w1", "vm0", "vm1"))
        def ev_k(mc, ps, Bps):
            mk.op("scalar", lambda e, m=mc, p=ps: e.activation(
                out=k_fm[:, m * CT:(m + 1) * CT], in_=p[:, 0:CT], func=AF.Identity,
                bias=bbk[:, m:m + 1]),
                reads=(Bps, mk.B("bbk_t")), writes=(mk.B("k_fm"),))
        proj(mk, mk.t["wst_bk"].ap(), "xt", ev_k, ld_eng="sync",
             w_rbuf=mk.B("wst_bk"), w_names=("w0", "w1", "vm0", "vm1"))
        for jc in range(4):
            for vfn in range(2):
                psn = ["psM0", "psM1", "psU", "psV"][(jc * 2 + vfn) % 4]
                ps = mk.t[psn]
                mk.op("tensor",
                      [(lambda e, dc=dinc, p=ps, j=jc, v=vfn: e.matmul(
                          p[:, 0:CT], xt[:, dc * CT + j * 128: dc * CT + j * 128 + 128],
                          wvm[:, dc * D + v * CT: dc * D + v * CT + CT],
                          start=(dc == 0), stop=(dc == FC - 1))) for dinc in range(FC)],
                      reads=(mk.B("xt"), mk.B("wvm")), writes=(mk.B(psn),))
                mk.op("scalar",
                      lambda e, j=jc, v=vfn, p=ps: e.activation(
                          out=vtm[:, j * D + v * CT: j * D + v * CT + CT],
                          in_=p[:, 0:CT], func=AF.Identity),
                      reads=(mk.B(psn),), writes=(mk.B("vtm"),))
        for jc in range(4):
            mk.op("tensor",
                  [(lambda e, kf=kfc, j=jc: e.transpose(
                      psT[:, kf * 128:(kf + 1) * 128],
                      k_fm[:, kf * CT + j * 128: kf * CT + j * 128 + 128],
                      eye[:, :])) for kfc in range(FC)],
                  reads=(mk.B("k_fm"), mk.B("eye_t")), writes=(mk.B("psT"),))
            mk.op("vector",
                  lambda e, j=jc, cc_=c: e.tensor_scalar_mul(
                      ktm[:, j * D:(j + 1) * D], psT[:, 0:D],
                      gk_t[:, cc_ * 4 + j: cc_ * 4 + j + 1]),
                  reads=(mk.B("psT"), mk.B("gk_t")), writes=(mk.B("ktm"),))
        mk.op("vector",
              [(lambda e, k=fc, cc_=c: e.tensor_mul(
                  qgs[:, k * CT:(k + 1) * CT], q_fm[:, k * CT:(k + 1) * CT],
                  gq_t[:, cc_ * CT:(cc_ + 1) * CT])) for fc in range(FC)],
              reads=(mk.B("q_fm"), mk.B("gq_t")), writes=(mk.B("qgs"),))
        if c < 2:
            mk.op("vector",
                  [(lambda e, k=fc, cc_=c: e.tensor_mul(
                      qgc[:, k * 2 * CT + cc_ * CT: k * 2 * CT + cc_ * CT + CT],
                      q_fm[:, k * CT:(k + 1) * CT],
                      gc_t[:, cc_ * CT:(cc_ + 1) * CT])) for fc in range(FC)],
                  reads=(mk.B("q_fm"), mk.B("gc_t")), writes=(mk.B("qgc"),))
        for jc in range(4):
            psn = ["psM0", "psM1", "psU", "psV"][jc % 4]
            ps = mk.t[psn]
            mk.op("tensor",
                  [(lambda e, kf=kfc, p=ps, j=jc: e.matmul(
                      p[:, 0:CT], k_fm[:, kf * CT + j * 128: kf * CT + j * 128 + 128],
                      q_fm[:, kf * CT:(kf + 1) * CT],
                      start=(kf == 0), stop=(kf == FC - 1))) for kfc in range(FC)],
                  reads=(mk.B("k_fm"), mk.B("q_fm")), writes=(mk.B(psn),))
            mk.op("vector",
                  lambda e, j=jc, p=ps: e.tensor_mul(
                      ptr[:, j * CT:(j + 1) * CT], p[:, 0:CT],
                      dmask[:, j * CT:(j + 1) * CT]),
                  reads=(mk.B(psn), mk.B("dmask_t")), writes=(mk.B("ptr"),))
        for vfc in range(FC):
            psn = ["psM0", "psM1", "psU", "psV"][vfc % 4]
            ps = mk.t[psn]
            fns = []
            for jc in range(4):
                fns.append(lambda e, j=jc, v=vfc, p=ps: e.matmul(
                    p[:, 0:CT], vtm[:, j * D + v * 128: j * D + v * 128 + 128],
                    ptr[:, j * CT:(j + 1) * CT], start=(j == 0), stop=False))
            for kfc in range(FC):
                fns.append(lambda e, kf=kfc, v=vfc, p=ps: e.matmul(
                    p[:, 0:CT], A_sb[:, kf * D + v * 128: kf * D + v * 128 + 128],
                    qgs[:, kf * CT:(kf + 1) * CT], start=False, stop=(kf == FC - 1)))
            mk.op("tensor", fns,
                  reads=(mk.B("vtm"), mk.B("ptr"), mk.B("A_sb"), mk.B("qgs")),
                  writes=(mk.B(psn),))
            s = slice(vfc * T + c * CT, vfc * T + c * CT + CT)
            mk.op("vector", lambda e, ss=s, p=ps: e.tensor_add(x_t[:, ss], x_t[:, ss],
                                                               p[:, 0:CT]),
                  reads=(mk.B(psn), Bx), writes=(Bx,))
        for kfc in range(FC):
            for vfn in range(2):
                psn = "psA" if vfn == 0 else "psB"
                ps = mk.t[psn]
                mk.op("tensor",
                      [(lambda e, j=jc, p=ps, kf=kfc, v=vfn: e.matmul(
                          p[:, 0:CT], ktm[:, j * D + kf * 128: j * D + kf * 128 + 128],
                          vtm[:, j * D + v * CT: j * D + v * CT + CT],
                          start=(j == 0), stop=(j == 3))) for jc in range(4)],
                      reads=(mk.B("ktm"), mk.B("vtm")), writes=(mk.B(psn),))
                s = slice(kfc * D + vfn * CT, kfc * D + vfn * CT + CT)
                mk.op("vector", lambda e, ss=s, p=ps: e.tensor_add(
                    A_sb[:, ss], A_sb[:, ss], p[:, 0:CT]),
                    reads=(mk.B(psn), mk.B("A_sb")), writes=(mk.B("A_sb"),))
        mk.dump_stream("bq", q_fm.ap()[:, :], FC * CT, c, BF16, reads=(mk.B("q_fm"),))
        mk.dump_stream("bptr", ptr.ap()[:, :], 4 * CT, c, BF16, reads=(mk.B("ptr"),))
        mk.dump_stream("bA", A_sb.ap()[:, :], FC * D, c, BF16, reads=(mk.B("A_sb"),))

    x8_in, x8_out, x8_sel = mk.t["x8_in"].ap(), mk.t["x8_out"].ap(), mk.t["x8_sel"].ap()
    b8 = mk.dma("gpsimd", x8_in[:, :], A_sb[:, :], reads=(mk.B("A_sb"),), key="xch8")
    cc8 = pair_exchange(mk, x8_in, x8_out)
    cc8.deps.add(b8)

    def x8_consume(eng, inc_args):
        for arm in eng.switch_core_id(n=128):
            if arm.logical % 2 == 1:
                eng.wait_ge(cc8.sem, 1)
                inst = eng.dma_start(out=x8_sel[:, :], in_=x8_out[0:128, :])
            else:
                inst = eng.dma_start(out=x8_sel[:, :], in_=x8_in[:, :])
            if inc_args:
                inst.then_inc(*inc_args)
    mk.switch("gpsimd", x8_consume, writes=(mk.B("x8_sel"),), deps=(b8,), key="sw8")

    for c in range(2):
        for half in range(2):
            for kfc in range(FC):
                an = f"Ain{kfc % 2}"
                a_t = Ain[kfc % 2]
                mk.dma("gpsimd", a_t[:, :], x8_sel[:, kfc * D:(kfc + 1) * D],
                       reads=(mk.B("x8_sel"),), writes=(mk.B(an),), key=an)
                for vi in range(4):
                    vfc = half * 4 + vi
                    psn = ["psM0", "psM1", "psA", "psB"][vi]
                    ps = mk.t[psn]
                    mk.op("tensor",
                          lambda e, kf=kfc, v=vfc, p=ps, at=a_t, cc_=c: e.matmul(
                              p[:, 0:CT], at[:, v * 128:(v + 1) * 128],
                              qgc[:, kf * 2 * CT + cc_ * CT: kf * 2 * CT + cc_ * CT + CT],
                              start=(kf == 0), stop=(kf == FC - 1)),
                          reads=(mk.B(an), mk.B("qgc")), writes=(mk.B(psn),))
            for vi in range(4):
                vfc = half * 4 + vi
                psn = ["psM0", "psM1", "psA", "psB"][vi]
                ps = mk.t[psn]
                s = slice(vfc * T + c * CT, vfc * T + c * CT + CT)
                mk.op("vector", lambda e, ss=s, p=ps: e.tensor_add(
                    x_t[:, ss], x_t[:, ss], p[:, 0:CT]),
                    reads=(mk.B(psn), Bx), writes=(Bx,))



def xf_ap(mk, fc):
    return (mk.t["xf03"], (fc % 4) * T) if fc < 4 else (mk.t["xf47"], (fc % 4) * T)


def xf_buf(mk, fc):
    return mk.B("xf03") if fc < 4 else mk.B("xf47")


def stage_weights(mk, W):
    jobs = []
    for nm, wn in (("wa_in", "ain"), ("wa_v", "av"), ("wa_be", "abe"),
                   ("wa_al", "aal"), ("wa_out", "aout"),
                   ("wb_q", "bq"), ("wb_k", "bk")):
        jobs.append((W[nm], mk.t[f"wst_{wn}"].ap(), 8, 1024))
    for l in range(2):
        jobs.append((W[f"wf_qkv{l}"], mk.t[f"wst_qkv{l}"].ap(), 16, 1024))
        jobs.append((W[f"wf_vm{l}"], mk.t[f"wst_vm{l}"].ap(), 8, 512))
        jobs.append((W[f"wf_o{l}"], mk.t[f"wst_o{l}"].ap(), 8, 512))
        jobs.append((W[f"wf_1{l}"], mk.t[f"wst_1{l}"].ap(), 44, 1024))
        jobs.append((W[f"wf_2{l}"], mk.t[f"wst_2{l}"].ap(), 22, 1024))
    for src, dst, n, cols in jobs:
        nm = dst.tensor.name
        dbuf = mk.B(nm)
        mk.dma("gpsimd", dst[:, :, :], src[:, :, :], writes=(dbuf,),
               key=f"st_{nm}")


def send_pair(mk, tag, src_ap, src_bufs, cols, dtype=F32, slow=False):
    in_d = mk.dram(f"xin_{tag}", [128, cols], dtype)
    out_d = mk.dram(f"xout_{tag}", [256, cols], dtype)
    b = mk.dma("gpsimd", in_d[:, :], src_ap, reads=tuple(src_bufs), key=f"xb_{tag}",
               slow=slow)
    cc = pair_exchange(mk, in_d, out_d)
    cc.deps.add(b)
    return cc, in_d, out_d, b


def recv_pair(mk, tag, cc, in_d, out_d, dst_ap, dst_bufs, dep):
    def consume(eng, inc_args):
        for arm in eng.switch_core_id(n=128):
            if arm.logical % 2 == 1:
                eng.wait_ge(cc.sem, 1)
                inst = eng.dma_start(out=dst_ap, in_=out_d[0:128, :])
            else:
                inst = eng.dma_start(out=dst_ap, in_=in_d[:, :])
            if inc_args:
                inst.then_inc(*inc_args)
    return mk.switch("gpsimd", consume, writes=tuple(dst_bufs), deps=(dep,),
                     key=f"sw_{tag}")


def build_ffn(mk, W, ci):
    x_t = mk.t["x_fm"]
    Bx = mk.B("x_fm")
    qin = mk.t["f_qin"]
    fq, fqs, fk, fks = mk.t["f_q"], mk.t["f_qs"], mk.t["f_k"], mk.t["f_ks"]
    fP, fsq = mk.t["f_P"], mk.t["f_sqv"]
    kw = [mk.t["f_ka"], mk.t["f_kb"]]
    vw = [mk.t["f_va"], mk.t["f_vb"]]
    k0, v0, fPT = mk.t["f_k0"], mk.t["f_v0"], mk.t["f_PT"]
    fo, frc = mk.t["f_o"], mk.t["f_rc"]
    nrm, cols = mk.t["psW"], mk.t["f_cols"]
    psT, psU, psV = mk.t["psT"], mk.t["psU"], mk.t["psV"]
    c0_t = mk.t["c0_t"]
    shb1, shb2, sb3 = mk.t["shb1"], mk.t["shb2"], mk.t["sb3"]
    eye, eps2 = mk.t["eye_t"], mk.t["eps2"]
    w0m, onesr = mk.t["w0m_t"], mk.t["onesr"]
    vm = [mk.t["vm0"], mk.t["vm1"]]
    gel = [mk.t["gel0"], mk.t["gel1"]]
    t47 = mk.t["xf47"]

    mk.dma("sync", w0m[0:1, :], W["w0m"][:, :], writes=(mk.B("w0m_t"),), key="w0md")
    hdv = mk.nc.inline_tensor(
        np.kron(np.eye(2), np.ones((64, 64))).astype(ml_dtypes.bfloat16), f"hd128_{ci}").ap()
    mk.dma("sync", mk.t["hdones"][:, :], hdv[:, :], writes=(mk.B("hdones"),), key="hdd")
    _p = np.arange(128)[:, None]
    _c = np.arange(128)[None, :]
    mtlv = mk.nc.inline_tensor(
        np.where(_c < _p, -1e30, 0.0).astype(ml_dtypes.bfloat16), f"mtl_{ci}").ap()
    mtrv = mk.nc.inline_tensor(
        np.where(_c > _p, -1e30, 0.0).astype(ml_dtypes.bfloat16), f"mtr_{ci}").ap()
    mk.dma("sync", mk.t["mtl"][:, :], mtlv[:, :], writes=(mk.B("mtl"),), key="mtld")
    mk.dma("sync", mk.t["mtr"][:, :], mtrv[:, :], writes=(mk.B("mtr"),), key="mtrd")
    for c in range(NCH):
        def dst_sl(fc, c_=c):
            t_, base = xf_ap(mk, fc)
            return t_[:, base + c_ * CT: base + c_ * CT + CT], xf_buf(mk, fc)
        layernorm(mk, "x_fm", c, None, dst_sl=dst_sl)

    def send_shift(tag):
        return send_pair(mk, tag, t47.ap()[:, T - 1::T], (mk.B("xf47"),), 4, slow=True)

    def dump_xf(tag):
        if ci != 0:
            return
        mk.dump_stream(tag + "a", mk.t["xf03"].ap()[:, :], 4 * T, 0, F32,
                       reads=(mk.B("xf03"),), n=1)
        mk.dump_stream(tag + "b", mk.t["xf47"].ap()[:, :], 4 * T, 0, F32,
                       reads=(mk.B("xf47"),), n=1)
    dump_xf("ff0")

    cc_s1, in_s1, out_s1, b_s1 = send_shift(f"s1c{ci}l0")

    for li in range(2):
        lw = {k: mk.t[f"wst_{k}{li}"].ap() for k in ("qkv", "vm", "o", "1", "2")}
        qskb = mk.t[f"qskb{li}"]
        lgp, lbp = mk.t[f"lgb{li}"], mk.t[f"lbb{li}"]
        b1p, b2p = mk.t[f"b1b{li}"], mk.t[f"b2b{li}"]

        if li == 1:
            cc_s1, in_s1, out_s1, b_s1 = send_shift(f"s1c{ci}l1")
        recv_pair(mk, f"s1c{ci}l{li}", cc_s1, in_s1, out_s1, shb1[:, 0:4],
                  (mk.B("shb1"),), b_s1)
        mk.op("vector", lambda e: e.tensor_scalar_mul(shb1[:, 4:8], shb1[:, 0:4],
                                                      c0_t[:, 0:1]),
              reads=(mk.B("shb1"), mk.B("c0_t")), writes=(mk.B("shb1"),))

        def build_qin(w):
            fns = []
            rbufs = {mk.B("xf03"), mk.B("xf47")}
            for fc in range(FC):
                t_, base = xf_ap(mk, fc)
                off = w * CT
                if fc < 4:
                    fns.append(lambda e, s=t_[:, base + off: base + off + CT],
                               d=qin[:, fc * CT:(fc + 1) * CT]: e.tensor_copy(d, s))
                elif w > 0:
                    fns.append(lambda e, s=t_[:, base + off - 1: base + off - 1 + CT],
                               d=qin[:, fc * CT:(fc + 1) * CT]: e.tensor_copy(d, s))
                else:
                    fns.append(lambda e, s=t_[:, base: base + CT - 1],
                               d=qin[:, fc * CT + 1:(fc + 1) * CT]: e.tensor_copy(d, s))
                    fns.append(lambda e, s=shb1[:, fc: fc + 1],
                               d=qin[:, fc * CT: fc * CT + 1]: e.tensor_copy(d, s))
            if w == 0:
                rbufs.add(mk.B("shb1"))
            mk.op("vector", fns, reads=tuple(rbufs),
                  writes=(mk.B("f_qin"), mk.B("f_P0"), mk.B("f_P1"),
                          mk.B("f_P2"), mk.B("f_P3")))

        def qkv(w, kdst, vdst, kdb, vdb, skip_qin=False):
            if not skip_qin:
                build_qin(w)
            PSV = ("psM0", "psM1", "psU", "psV")
            VW = ("vm0", "vm1", "w0", "w1")
            for kc in range(FC):
                vmn = VW[kc % 4]
                vt = mk.t[vmn]
                mk.dma("sync", vt[:, 0:CT], lw["vm"][kc, :, :],
                       writes=(mk.B(vmn),), key=vmn)
                mk.op("tensor",
                      [(lambda e, k_=kc, j=jc_, vb_=vt: e.matmul(
                          mk.t[PSV[j]][:, 0:CT],
                          qin[:, k_ * CT + j * 128: k_ * CT + j * 128 + 128],
                          vb_[:, 0:CT], start=(k_ == 0), stop=(k_ == FC - 1)))
                       for jc_ in range(4)],
                      reads=(mk.B(vmn), mk.B("f_qin")),
                      writes=tuple(mk.B(p) for p in PSV))
            for jc in range(4):
                mk.op("scalar", lambda e, j=jc, p=mk.t[PSV[jc]]: e.activation(
                    out=vdst[:, j * CT:(j + 1) * CT], in_=p[:, 0:CT], func=AF.Identity),
                    reads=(mk.B(PSV[jc]),), writes=(vdb,))
            def ev(mc, ps, Bps):
                tgt, tb = [(fq, "f_q"), (fqs, "f_qs"), (fk, "f_k"), (fks, "f_ks")][mc // 4]
                j = mc % 4
                mk.op("scalar", lambda e, p=ps, t_=tgt, jj=j: e.activation(
                    out=t_[:, jj * CT:(jj + 1) * CT], in_=p[:, 0:CT], func=AF.Identity),
                    reads=(Bps,), writes=(mk.B(tb),))
            proj(mk, lw["qkv"], "f_qin", ev, n_mc=16, ld_eng="sync",
                 w_names=("w0", "w1", "vm0", "vm1"))
            if ci == 0 and li == 0:
                mk.dump_stream("fqraw", fq.ap()[:, :], 4 * CT, w, BF16, reads=(mk.B("f_q"),))
                mk.dump_stream("fqsraw", fqs.ap()[:, :], 4 * CT, w, BF16, reads=(mk.B("f_qs"),))
            mk.dma("sync", frc.ap().rearrange("p (a c) -> p a c", a=4),
                   W["rot"][:, :, w * CT:(w + 1) * CT], writes=(mk.B("f_rc"),), key="frc")
            mk.op("vector",
                  [(lambda e, i_=i, qb=qskb: e.tensor_scalar_mul(
                      frc[:, i_ * CT:(i_ + 1) * CT], frc[:, i_ * CT:(i_ + 1) * CT],
                      qb[:, i_:i_ + 1])) for i in range(4)],
                  reads=(mk.B("f_rc"), mk.B(f"qskb{li}")), writes=(mk.B("f_rc"),))
            PRL = ((fq, fqs, "f_q", "f_qs", "f_sqv", ("psM0", "psM1"), "f_rr0"),
                   (fk, fks, "f_k", "f_ks", "f_sqv2", ("psU", "psV"), "f_rr1"))

            def l2_muls(j, pi):
                a_, b_, an, bn, _sq, _ps, rrn = PRL[pi]
                rslot = mk.t[rrn]
                mk.op("vector",
                      [(lambda e, j_=j, t=a_: e.tensor_mul(
                          t[:, j_ * CT:(j_ + 1) * CT], t[:, j_ * CT:(j_ + 1) * CT],
                          rslot[:, :])),
                       (lambda e, j_=j, t=b_: e.tensor_mul(
                          t[:, j_ * CT:(j_ + 1) * CT], t[:, j_ * CT:(j_ + 1) * CT],
                          rslot[:, :]))],
                      reads=(mk.B(an), mk.B(bn), mk.B(rrn)),
                      writes=(mk.B(an), mk.B(bn)))

            for j_ in range(4):
                for pi_ in range(2):
                    a_, b_, an, bn, sqn, psns, rrn = PRL[pi_]
                    sqt = mk.t[sqn]
                    psnn = psns[j_ % 2]
                    psn_t = mk.t[psnn]
                    mk.op("vector", lambda e, j=j_, t=a_, sq_=sqt: e.tensor_mul(
                        sq_[:, 0:CT], t[:, j * CT:(j + 1) * CT],
                        t[:, j * CT:(j + 1) * CT]),
                        reads=(mk.B(an),), writes=(mk.B(sqn),))
                    mk.op("tensor", lambda e, p=psn_t, sq_=sqt: e.matmul(
                        p[:, 0:CT], mk.t["hdones"][:, :], sq_[:, 0:CT],
                        start=True, stop=True),
                        reads=(mk.B(sqn), mk.B("hdones")), writes=(mk.B(psnn),))
                if j_ > 0:
                    l2_muls(j_ - 1, 0)
                    l2_muls(j_ - 1, 1)
                for pi_ in range(2):
                    a_, b_, an, bn, sqn, psns, rrn = PRL[pi_]
                    psnn = psns[j_ % 2]
                    psn_t = mk.t[psnn]
                    mk.op("scalar", lambda e, p=psn_t: e.activation(
                        out=nrm[:, 0:CT], in_=p[:, 0:CT], func=AF.Ln,
                        bias=eps2[:, 0:1]),
                        reads=(mk.B(psnn), mk.B("eps2")), writes=(mk.B("psW"),))
                    mk.op("scalar", lambda e, rs=mk.t[rrn]: e.activation(
                        out=rs[:, :], in_=nrm[:, 0:CT], func=AF.Exp, scale=-0.5),
                        reads=(mk.B("psW"),), writes=(mk.B(rrn),))
            l2_muls(3, 0)
            l2_muls(3, 1)
            for (a_, b_, an, bn, ai, bi) in ((fq, fqs, "f_q", "f_qs", 0, 1),
                                             (fk, fks, "f_k", "f_ks", 2, 3)):
                if ci == 0 and li == 0 and an == "f_q":
                    mk.dump_stream("fqn", fq.ap()[:, :], 4 * CT, w, BF16, reads=(mk.B("f_q"),))
                    mk.dump_stream("fqsn", fqs.ap()[:, :], 4 * CT, w, BF16, reads=(mk.B("f_qs"),))
                    mk.dump_stream("frcd", frc.ap()[:, :], 4 * CT, w, BF16, reads=(mk.B("f_rc"),))
                mk.op("vector",
                      [(lambda e, j=j_, t=b_, i_=bi: e.tensor_mul(
                          t[:, j * CT:(j + 1) * CT], t[:, j * CT:(j + 1) * CT],
                          frc[:, i_ * CT:(i_ + 1) * CT])) for j_ in range(4)],
                      reads=(mk.B(bn), mk.B("f_rc")), writes=(mk.B(bn),))
                mk.op("vector",
                      [(lambda e, j=j_, t=a_, i_=ai: e.tensor_mul(
                          t[:, j * CT:(j + 1) * CT], t[:, j * CT:(j + 1) * CT],
                          frc[:, i_ * CT:(i_ + 1) * CT])) for j_ in range(4)],
                      reads=(mk.B(an), mk.B("f_rc")), writes=(mk.B(an),))
                mk.op("vector",
                      [(lambda e, j=j_, t=a_, t2=b_: e.tensor_add(
                          t[:, j * CT:(j + 1) * CT], t[:, j * CT:(j + 1) * CT],
                          t2[:, j * CT:(j + 1) * CT])) for j_ in range(4)],
                      reads=(mk.B(an), mk.B(bn)), writes=(mk.B(an),))
            mk.op("vector", lambda e: e.tensor_copy(kdst[:, :], fk[:, :]),
                  reads=(mk.B("f_k"),), writes=(kdb,))
            if ci == 0 and li == 0:
                mk.dump_stream("fq", fq.ap()[:, :], 4 * CT, w, BF16, reads=(mk.B("f_q"),))
                mk.dump_stream("fk", fk.ap()[:, :], 4 * CT, w, BF16, reads=(mk.B("f_k"),))
                mk.dump_stream("fv", vdst.ap()[:, :] if hasattr(vdst, "ap") else vdst[:, :],
                               4 * CT, w, BF16, reads=(vdb,))

        def attn(w, kprev, vprev, kpb, vpb, kcur, vcur, kcb, vcb, pre_wo=None):
            PAIRS3 = (("psM0", "psM1"), ("psU", "psV"), ("psB", "psW"))
            psu = mk.t["psA"]

            def pslot(hh, ic):
                s = (hh % 2) * 2 + (ic % 2)
                return mk.t[f"f_P{s}"], mk.B(f"f_P{s}")

            def colsb(hh, ic):
                if hh % 2 == 0:
                    return cols, (ic % 2) * 4, mk.B("f_cols")
                return mk.t["f_colsB"], (ic % 2) * 4, mk.B("f_colsB")

            def ptslot(hh, ic):
                s = (hh % 2) * 2 + (ic % 2)
                return fPT[:, s * 640: s * 640 + 640]

            def smm_part(hh, ic, n, hp):
                pspn, pscn = PAIRS3[n % 3]
                psp, psc = mk.t[pspn], mk.t[pscn]
                hrow = 64 * (hh % 2)
                npv = CT - ic * 128
                ncr = ic * 128 + 128
                qsl = fq[hrow:hrow + 64, hp * CT + ic * 128: hp * CT + ic * 128 + 128]
                kp = kprev[hrow:hrow + 64, hp * CT + ic * 128: hp * CT + CT]
                kc_ = kcur[hrow:hrow + 64, hp * CT: hp * CT + ncr]
                if w == 0:
                    mk.op("tensor",
                          [(lambda e, q=qsl, k_=kp, n_=npv, p=psp: e.matmul(
                              p[:, 0:n_], q, k_, start=True, stop=False)),
                           (lambda e, n_=npv, p=psp: e.matmul(
                              p[:, 0:n_], onesr[0:1, 0:128], w0m[0:1, 0:n_],
                              start=False, stop=True))],
                          reads=(kpb, mk.B("f_q"), mk.B("onesr"), mk.B("w0m_t")),
                          writes=(mk.B(pspn),))
                else:
                    mk.op("tensor", lambda e, q=qsl, k_=kp, n_=npv, p=psp: e.matmul(
                        p[:, 0:n_], q, k_, start=True, stop=True),
                        reads=(kpb, mk.B("f_q")), writes=(mk.B(pspn),))
                mk.op("tensor", lambda e, q=qsl, k_=kc_, n_=ncr, p=psc: e.matmul(
                    p[:, 0:n_], q, k_, start=True, stop=True),
                    reads=(kcb, mk.B("f_q")), writes=(mk.B(pscn),))

            def chain_part(hh, ic, n):
                pspn, pscn = PAIRS3[n % 3]
                psp, psc = mk.t[pspn], mk.t[pscn]
                npv = CT - ic * 128
                ncr = ic * 128 + 128
                cl, cb, Bcl = colsb(hh, ic)
                mk.op("vector", lambda e, p=psp: e.tensor_add(
                    p[:, 0:128], p[:, 0:128], mk.t["mtl"][:, :]),
                    reads=(mk.B(pspn), mk.B("mtl")), writes=(mk.B(pspn),))
                mk.op("vector", lambda e, n_=ncr, p=psc: e.tensor_add(
                    p[:, n_ - 128:n_], p[:, n_ - 128:n_], mk.t["mtr"][:, :]),
                    reads=(mk.B(pscn), mk.B("mtr")), writes=(mk.B(pscn),))
                mk.op("vector",
                      [(lambda e, n_=npv, p=psp, c=cb, cc=cl: e.reduce_max(
                          cc[:, c + 1:c + 2], p[:, 0:n_], axis=AX.X)),
                       (lambda e, n_=ncr, p=psc, c=cb, cc=cl: e.reduce_max(
                          cc[:, c + 2:c + 3], p[:, 0:n_], axis=AX.X))],
                      reads=(mk.B(pspn), mk.B(pscn)), writes=(Bcl,))
                mk.op("vector", lambda e, c=cb, cc=cl: e.reduce_max(
                    cc[:, c:c + 1], cc[:, c + 1:c + 3], axis=AX.X, negate=True),
                    reads=(Bcl,), writes=(Bcl,))
                Pb, Bp = pslot(hh, ic)
                mk.op("scalar", lambda e, n_=npv, p=psp, P_=Pb, c=cb, cc=cl: e.activation(
                          out=P_[:, 0:n_], in_=p[:, 0:n_], func=AF.Exp,
                          bias=cc[:, c:c + 1], accum_out=cc[:, c + 1:c + 2]),
                      reads=(mk.B(pspn), Bcl),
                      writes=(Bp, mk.B("f_qin"), Bcl))
                mk.op("scalar", lambda e, n_=ncr, o=npv, p=psc, P_=Pb, c=cb, cc=cl: e.activation(
                          out=P_[:, o:o + n_], in_=p[:, 0:n_], func=AF.Exp,
                          bias=cc[:, c:c + 1], accum_out=cc[:, c + 2:c + 3]),
                      reads=(mk.B(pscn), Bcl),
                      writes=(Bp, mk.B("f_qin"), Bcl))

            def den_part(hh, ic):
                cl, cb, Bcl = colsb(hh, ic)
                Pb, Bp = pslot(hh, ic)
                mk.op("vector", lambda e, c=cb, cc=cl: e.tensor_add(
                    cc[:, c + 1:c + 2], cc[:, c + 1:c + 2], cc[:, c + 2:c + 3]),
                    reads=(Bcl,), writes=(Bcl,))
                mk.op("vector", lambda e, c=cb, cc=cl: e.reciprocal(
                    cc[:, c + 3:c + 4], cc[:, c + 1:c + 2]),
                    reads=(Bcl,), writes=(Bcl,))
                mk.op("vector", lambda e, P_=Pb, c=cb, cc=cl: e.tensor_scalar_mul(
                    P_[:, 0:640], P_[:, 0:640], cc[:, c + 3:c + 4]),
                    reads=(Bp, Bcl), writes=(Bp,))

            def t_part(hh, ic):
                Pb, Bp = pslot(hh, ic)
                PTb = ptslot(hh, ic)
                mk.op("tensor",
                      [(lambda e, j=jcb, sc=(jcb - ic) * 128, P_=Pb: e.transpose(
                          psT[:, j * 128:(j + 1) * 128], P_[:, sc: sc + 128],
                          eye[:, :])) for jcb in range(ic, ic + 5)],
                      reads=(Bp, mk.B("eye_t")), writes=(mk.B("psT"),))
                mk.op("scalar",
                      [(lambda e, j=jcb, d=PTb, sc=(jcb - ic) * 128: e.activation(
                          out=d[:, sc: sc + 128],
                          in_=psT[:, j * 128:(j + 1) * 128], func=AF.Identity))
                       for jcb in range(ic, ic + 5)],
                      reads=(mk.B("psT"),), writes=(mk.B("f_PT"),))

            def pv_part(hh, ic):
                hrow = 64 * (hh % 2)
                PTb = ptslot(hh, ic)
                fns = []
                for jj in range(5):
                    j = ic + jj
                    vsrc = vprev if j < 4 else vcur
                    va = vsrc[:, (j % 4) * CT + hh * 64: (j % 4) * CT + hh * 64 + 64]
                    fns.append(lambda e, v_=va, jx=jj, hr=hrow, i_=ic, d=PTb: e.matmul(
                        psu[hr:hr + 64, i_ * 128:i_ * 128 + 128], v_,
                        d[:, jx * 128: jx * 128 + 128],
                        start=(jx == 0), stop=(jx == 4)))
                mk.op("tensor", fns, reads=(vpb, vcb, mk.B("f_PT")),
                      writes=(mk.B("psA"),))

            seq = [(hp_, ic_) for hp_ in range(4) for ic_ in range(4)]
            for idx in range(len(seq) + 2):
                cur = seq[idx] if idx < len(seq) else None
                prv = seq[idx - 2] if idx >= 2 else None
                for k_ in range(2):
                    if prv is not None:
                        t_part(2 * prv[0] + k_, prv[1])
                    if cur is not None:
                        smm_part(2 * cur[0] + k_, cur[1], idx * 2 + k_, cur[0])
                if cur is not None:
                    chain_part(2 * cur[0], cur[1], idx * 2)
                    chain_part(2 * cur[0] + 1, cur[1], idx * 2 + 1)
                if prv is not None:
                    pv_part(2 * prv[0], prv[1])
                    pv_part(2 * prv[0] + 1, prv[1])
                if cur is not None:
                    den_part(2 * cur[0], cur[1])
                    den_part(2 * cur[0] + 1, cur[1])
                if prv is not None and prv[1] == 3:
                    mk.op("vector", lambda e, t0=prv[0] * CT: e.tensor_copy(
                        fo[:, t0:t0 + CT], psu[:, 0:CT]),
                        reads=(mk.B("psA"),), writes=(mk.B("f_o"),))
            def ev_o(mc, ps, Bps):
                t_, base = xf_ap(mk, mc)
                sl = slice(base + w * CT, base + w * CT + CT)
                mk.op("vector", lambda e, p=ps, ss=sl, tt=t_, m=mc: e.tensor_add(
                    tt[:, ss], tt[:, ss], p[:, 0:CT]),
                    reads=(Bps, xf_buf(mk, mc)), writes=(xf_buf(mk, mc),))
            if ci == 0 and li == 0:
                mk.dump_stream("fo", fo.ap()[:, :], 4 * CT, w, BF16, reads=(mk.B("f_o"),))
            if pre_wo is not None:
                pre_wo()
            proj(mk, lw["o"], "f_o", ev_o, n_mc=FC, kc_n=4, ld_eng="sync",
                 w_names=("w0", "w1", "vm0", "vm1"))

        qkv(0, k0, v0, mk.B("f_k0"), mk.B("f_v0"))
        q0d = mk.dram(f"q0_{ci}{li}", [128, 4 * CT], BF16)
        q0sp = mk.dma("sync", q0d[:, :], fq.ap()[:, :], reads=(mk.B("f_q"),),
                      key=f"q0sp{li}")
        qkv(1, kw[1], vw[1], mk.B("f_kb"), mk.B("f_vb"))
        attn(1, k0, v0, mk.B("f_k0"), mk.B("f_v0"),
             kw[1], vw[1], mk.B("f_kb"), mk.B("f_vb"),
             pre_wo=lambda: build_qin(2))
        qkv(2, kw[0], vw[0], mk.B("f_ka"), mk.B("f_va"), skip_qin=True)
        attn(2, kw[1], vw[1], mk.B("f_kb"), mk.B("f_vb"),
             kw[0], vw[0], mk.B("f_ka"), mk.B("f_va"),
             pre_wo=lambda: build_qin(3))
        qkv(3, kw[1], vw[1], mk.B("f_kb"), mk.B("f_vb"), skip_qin=True)
        attn(3, kw[0], vw[0], mk.B("f_ka"), mk.B("f_va"),
             kw[1], vw[1], mk.B("f_kb"), mk.B("f_vb"))
        bb16 = mk.t["Bb"].ap().bitcast(BF16)
        cc_kv, in_kv, out_kv, b_kv = send_pair(
            mk, f"kv{ci}l{li}", bb16[:, 4096:8192], (mk.B("f_kb"), mk.B("f_vb")),
            4096, BF16)
        recv_pair(mk, f"kv{ci}l{li}", cc_kv, in_kv, out_kv, bb16[:, 0:4096],
                  (mk.B("f_ka"), mk.B("f_va")), b_kv)
        mk.dma("sync", fq.ap()[:, :], q0d[:, :], writes=(mk.B("f_q"),),
               deps=(q0sp,), key=f"q0ld{li}")
        attn(0, kw[0], vw[0], mk.B("f_ka"), mk.B("f_va"),
             k0, v0, mk.B("f_k0"), mk.B("f_v0"))
        dump_xf(f"ffw{li}")
        mk.barrier()
        mk.dma("sync", mk.t[f"b1b{li}"][:, :], W[f"b1{li}"][:, :],
               writes=(mk.B(f"b1b{li}"),), key=f"bb{li}")

        cc_s2, in_s2, out_s2, b_s2 = send_shift(f"s2c{ci}l{li}")
        recv_pair(mk, f"s2c{ci}l{li}", cc_s2, in_s2, out_s2, shb2[:, 0:4],
                  (mk.B("shb2"),), b_s2)
        mk.op("vector", lambda e: e.tensor_scalar_mul(shb2[:, 4:8], shb2[:, 0:4],
                                                      c0_t[:, 0:1]),
              reads=(mk.B("shb2"), mk.B("c0_t")), writes=(mk.B("shb2"),))
        mk.op("vector",
              [(lambda e, k=fc: e.tensor_copy(
                  sb3[:, (k - 4) * 3:(k - 4) * 3 + 3],
                  t47[:, (k % 4) * T + CT - 1: (k % 4) * T + 3 * CT: CT]))
               for fc in range(4, FC)],
              reads=(mk.B("xf47"),), writes=(mk.B("sb3"),))

        hg1, hg2 = mk.t["f_hg1"], mk.t["f_hg2"]
        def hg_sl(kc):
            if kc < 16:
                return hg1[:, kc * CT:(kc + 1) * CT], mk.B("Bb")
            return hg2[:, (kc - 16) * CT:(kc - 16 + 1) * CT], mk.B("f_hg2")

        for sl_i in (1, 2, 3, 0):
            t0 = sl_i * CT
            def seg(fc, s_=sl_i):
                t_, base = xf_ap(mk, fc)
                o = s_ * CT
                if fc < 4:
                    return [(0, CT, t_[:, base + o: base + o + CT])]
                return [(1, CT - 1, t_[:, base + o: base + o + CT - 1])]
            if sl_i == 0:
                patches = [(fc, shb2[:, fc: fc + 1], mk.B("shb2"))
                           for fc in range(4, FC)]
            else:
                patches = [(fc, sb3[:, (fc - 4) * 3 + sl_i - 1: (fc - 4) * 3 + sl_i],
                            mk.B("sb3")) for fc in range(4, FC)]
            layernorm(mk, None, sl_i, "xt", src_seg=seg,
                      src_bufs=(mk.B("xf03"), mk.B("xf47")),
                      patches=patches, affine=(lgp, lbp, (mk.B(f"lgb{li}"), mk.B(f"lbb{li}"))),
                      u_name="ln_u2")
            for i_ in range(22):
                pa, pb = (("psM0", "psM1") if i_ % 2 == 0 else ("psU", "psV"))
                wna, wnb = ("w0", "w1") if i_ % 2 == 0 else ("vm0", "vm1")
                for wn, psn, mc in ((wna, pa, i_), (wnb, pb, 22 + i_)):
                    wb = mk.t[wn]
                    mk.dma("sync", wb[:, :], lw["1"][mc, :, :], writes=(mk.B(wn),), key=wn)
                    ps = mk.t[psn]
                    mk.op("tensor",
                          [(lambda e, k_=kc, p=ps, w_=wb: e.matmul(
                              p[:, 0:CT], w_[:, k_ * 128:(k_ + 1) * 128],
                              mk.t["xt"][:, k_ * CT:(k_ + 1) * CT],
                              start=(k_ == 0), stop=(k_ == FC - 1))) for kc in range(FC)],
                          reads=(mk.B(wn), mk.B("xt")), writes=(mk.B(psn),))
                g = gel[i_ % 2]
                gn = "xp"
                mk.op("scalar", lambda e, i__=i_, gg=g, bp=b1p, p=mk.t[pb]: e.activation(
                    out=gg[:, :], in_=p[:, 0:CT], func=AF.Gelu,
                    bias=bp[:, 22 + i__: 23 + i__]),
                    reads=(mk.B(pb), mk.B(f"b1b{li}")), writes=(mk.B(gn),))
                hga, hgb = hg_sl(i_)
                mk.op("vector", lambda e, i__=i_, gg=g, d=hga, bp=b1p, p=mk.t[pa]:
                      e.scalar_tensor_tensor(
                          d, p[:, 0:CT], bp[:, i__:i__ + 1], gg[:, :], ALU.add, ALU.mult),
                      reads=(mk.B(pa), mk.B(gn), mk.B(f"b1b{li}")), writes=(hgb,))
            pss = ["psA", "psB", "psM0", "psM1"]
            for half in range(2):
                for kc in range(22):
                    wn = ("w0", "w1", "vm0", "vm1")[kc % 4]
                    wb = mk.t[wn]
                    mk.dma("sync", wb[:, 0:CT], lw["2"][kc, :, half * CT:(half + 1) * CT],
                           writes=(mk.B(wn),), key=wn)
                    hga, hgb = hg_sl(kc)
                    mk.op("tensor",
                          [(lambda e, k_=kc, v_=vi, w_=wb, hh=hga: e.matmul(
                              mk.t[pss[v_]][:, 0:CT], w_[:, v_ * 128:(v_ + 1) * 128], hh,
                              start=(k_ == 0), stop=(k_ == 21))) for vi in range(4)],
                          reads=(mk.B(wn), hgb),
                          writes=tuple(mk.B(p) for p in pss))
                for vi in range(4):
                    dm = half * 4 + vi
                    t_, base = xf_ap(mk, dm)
                    ps = mk.t[pss[vi]]
                    sl = slice(base + t0, base + t0 + CT)
                    mk.op("vector",
                          lambda e, p=ps, ss=sl, tt=t_, d_=dm, bp=b2p: e.scalar_tensor_tensor(
                              tt[:, ss], p[:, 0:CT], bp[:, d_:d_ + 1], tt[:, ss],
                              ALU.add, ALU.add),
                          reads=(mk.B(pss[vi]), xf_buf(mk, dm), mk.B(f"b2b{li}")),
                          writes=(xf_buf(mk, dm),))
        dump_xf(f"ffs{li}")
        mk.barrier()

    fns = []
    for fc in range(FC):
        t_, base = xf_ap(mk, fc)
        fns.append(lambda e, tt=t_, b_=base, k=fc: e.tensor_add(
            x_t[:, k * T:(k + 1) * T], x_t[:, k * T:(k + 1) * T], tt[:, b_: b_ + T]))
    mk.op("vector", fns, reads=(mk.B("xf03"), mk.B("xf47"), Bx), writes=(Bx,))

def build(dbg_names=(), parts=("a", "f", "b")):
    import contextlib
    mk = MK(dbg_names)
    stack = contextlib.ExitStack()
    mk._exit = stack
    nc = mk.nc

    W = {}
    x_in = mk.din("x", [128, FC * T])
    for n in ("wa_in", "wa_al", "wa_be", "wa_v", "wa_out", "wb_q", "wb_k"):
        W[n] = mk.din(n, [FC, 128, D])
    W["wb_v"] = mk.din("wb_v", [FC, 128, D])
    for n, key in (("bal", "ba_al"), ("bbe", "ba_be"), ("bvv", "ba_v"),
                   ("bbq_t", "bb_q"), ("bbk_t", "bb_k")):
        W[key] = mk.din(key, [128, FC])
    c0_in = mk.din("c0", [128, 1])
    W["gk"] = mk.din("gk", [128, 4 * NCH])
    for l in range(2):
        W[f"wf_qkv{l}"] = mk.din(f"wf_qkv{l}", [16, 128, 1024])
        W[f"wf_vm{l}"] = mk.din(f"wf_vm{l}", [8, 128, 512])
        W[f"wf_o{l}"] = mk.din(f"wf_o{l}", [8, 128, 512])
        W[f"wf_1{l}"] = mk.din(f"wf_1{l}", [44, 128, 1024])
        W[f"wf_2{l}"] = mk.din(f"wf_2{l}", [22, 128, 1024])
        W[f"qsk{l}"] = mk.din(f"qsk{l}", [128, 4])
        W[f"lg{l}"] = mk.din(f"lg{l}", [128, FC])
        W[f"lb{l}"] = mk.din(f"lb{l}", [128, FC])
        W[f"b1{l}"] = mk.din(f"b1{l}", [128, 44])
        W[f"b2{l}"] = mk.din(f"b2{l}", [128, FC])
    W["rot"] = mk.din("rot", [128, 4, T], BF16)
    W["w0m"] = mk.din("w0m", [1, CT], BF16)
    W["gq"] = mk.din("gq", [128, T], BF16)
    gc_in = mk.din("gc", [128, 2 * CT], BF16)
    dm_in = mk.din("dmask", [128, 4 * CT], BF16)
    out_x = mk.dout("out", [128, FC * T])

    mk.t["x1_in"] = View(mk.dram("x1_in", [128, FC]))
    mk.t["x1_out"] = View(mk.dram("x1_out", [256, FC]))
    mk.t["x8_in"] = View(mk.dram("x8_in", [128, FC * D], BF16))
    mk.t["x8_out"] = View(mk.dram("x8_out", [256, FC * D], BF16))
    mk._reg("x8_sel", mk.dram("x8_sel", [128, FC * D], BF16), None)
    for l in range(2):
        mk._reg(f"wst_qkv{l}", mk.dram(f"wst_qkv{l}", [16, 128, 1024], BF16), None)
        mk._reg(f"wst_vm{l}", mk.dram(f"wst_vm{l}", [8, 128, 512], BF16), None)
        mk._reg(f"wst_o{l}", mk.dram(f"wst_o{l}", [8, 128, 512], BF16), None)
        mk._reg(f"wst_1{l}", mk.dram(f"wst_1{l}", [44, 128, 1024], BF16), None)
        mk._reg(f"wst_2{l}", mk.dram(f"wst_2{l}", [22, 128, 1024], BF16), None)
    for _n in ("ain", "av", "abe", "aal", "aout", "bq", "bk"):
        mk._reg(f"wst_{_n}", mk.dram(f"wst_{_n}", [8, 128, 1024], BF16), None)

    mk.sb("x_fm", [128, FC * T], F32)
    mk.sb("xt", [128, FC * CT], BF16)
    mk.sb("xp", [128, FC * CT], BF16)
    mk.sb("Vb", [128, FC * CT], F32)
    mk.sb("Bb", [128, FC * CT], F32)
    mk.sb("hch", [128, FC * CT], F32)
    mk.sb("hfull", [128, FC * T], BF16)
    mk.sb("pfull", [128, FC * T], BF16)
    mk.sb("w0", [128, FC * 128], BF16)
    mk.sb("w1", [128, FC * 128], BF16)
    for n in ("hcar", "pcar", "hin", "hinm"):
        mk.sb(n, [128, FC], F32)
    mk.sb("c0_t", [128, 1], F32)
    mk.sb("zb", [128, CT], F32)
    mk.sb("ones_bf", [128, 128], BF16)
    mk.sb("eps", [128, 1], F32)
    for n in ("bal", "bbe", "bvv", "bbq_t", "bbk_t"):
        mk.sb(n, [128, FC], F32)
    mk.view("ln_xb", "xt", BF16, 0, FC * CT, share="xt")
    mk.view("ln_sq", "xp", BF16, 0, FC * CT, share="xp")
    mk.view("ln_u", "Bb", F32, 0, FC * CT, share="Bb")
    mk.sb("ln_m", [128, CT], F32)
    mk.sb("ln_msq", [128, CT], F32)
    mk.view("ln_r", "ln_msq", F32, 0, CT, share="ln_msq")
    mk.view("tmpS", "Vb", F32, 0, FC * CT, share="Vb")
    mk.view("q_fm", "Vb", BF16, 0, FC * CT)
    mk.view("ptr", "Vb", BF16, FC * CT, 4 * CT)
    mk.view("k_fm", "Bb", BF16, 0, FC * CT, share="Bb")
    mk.view("ktm", "Bb", BF16, FC * CT, 4 * D, share="Bb")
    mk.view("vtm", "hch", BF16, 0, 4 * D)
    mk.view("qgs", "xp", BF16, 0, FC * CT, share="xp")
    mk.view("qgc", "hfull", BF16, 0, FC * 2 * CT)
    mk.view("wvm", "hfull", BF16, FC * 2 * CT, FC * D)
    mk.view("A_sb", "pfull", BF16, 0, FC * D)
    mk.view("gq_t", "pfull", BF16, FC * D, T)
    mk.view("dmask_t", "pfull", BF16, FC * D + T, 4 * CT)
    mk.view("gc_t", "pfull", BF16, FC * D + T + 4 * CT, 2 * CT)
    mk.view("gk_t", "pfull", F32, (FC * D + T + 6 * CT) // 2, 4 * NCH)
    mk.sb("eye_t", [128, 128], BF16)
    mk.view("Ain0", "hch", BF16, 4 * D, D)
    mk.view("Ain1", "hch", BF16, 5 * D, D)
    mk.view("xf03", "hfull", F32, 0, 4 * T)
    mk.view("xf47", "pfull", F32, 0, 4 * T)
    mk.view("f_qin", "xt", BF16, 0, FC * CT, share="xt")
    mk.view("f_P", "xt", BF16, 0, 4 * 640, share="xt")
    for _s in range(4):
        mk.view(f"f_P{_s}", "xt", BF16, _s * 640, 640)
    mk.view("f_sqv", "xt", BF16, 4 * 640, CT, share="xt")
    mk.view("f_sqv2", "xt", BF16, 4 * 640 + CT, CT, share="xt")
    mk.view("f_q", "Vb", BF16, 0, 4 * CT)
    mk.view("f_qs", "Vb", BF16, 4 * CT, 4 * CT)
    mk.view("f_k", "Vb", BF16, 8 * CT, 4 * CT)
    mk.view("f_ks", "Vb", BF16, 12 * CT, 4 * CT)
    mk.view("ln_u2", "Vb", F32, 0, FC * CT)
    mk.view("f_ka", "Bb", BF16, 0, 4 * CT)
    mk.view("f_va", "Bb", BF16, 4 * CT, 4 * CT)
    mk.view("f_kb", "Bb", BF16, 8 * CT, 4 * CT)
    mk.view("f_vb", "Bb", BF16, 12 * CT, 4 * CT)
    mk.view("f_hg1", "Bb", BF16, 0, FC * 2 * CT, share="Bb")
    mk.view("f_k0", "hch", BF16, 0, 4 * CT)
    mk.view("f_v0", "hch", BF16, 4 * CT, 4 * CT)
    mk.view("f_PT", "hch", BF16, 8 * CT, 4 * 640)
    mk.view("f_hg2", "hch", BF16, 0, 6 * CT)
    mk.view("f_o", "xp", BF16, 0, 4 * CT, share="xp")
    mk.view("f_rc", "xp", BF16, 4 * CT, 4 * CT, share="xp")
    mk.view("f_rr0", "hch", BF16, 6656, CT)
    mk.view("f_rr1", "hch", BF16, 6656 + CT, CT)
    mk.sb("f_cols", [128, 16], F32)
    mk.view("f_colsB", "f_cols", F32, 8, 8)
    mk.sb("shb1", [128, 8], F32)
    mk.sb("shb2", [128, 8], F32)
    mk.sb("sb3", [128, 12], F32)
    mk.sb("vm0", [128, 1024], BF16)
    mk.sb("vm1", [128, 1024], BF16)
    mk.view("gel0", "xp", BF16, 0, CT, share="xp")
    mk.view("gel1", "xp", BF16, CT, CT, share="xp")
    mk.view("hdones", "zb", BF16, CT, 128)
    mk.view("mtl", "zb", BF16, CT + 128, 128)
    mk.view("mtr", "zb", BF16, CT + 256, 128)
    mk.sb("eps2", [128, 1], F32)
    mk.sb("onesr", [1, 128], BF16)
    mk.view("w0m_t", "zb", BF16, 0, CT)
    for l in range(2):
        mk.sb(f"qskb{l}", [128, 4], F32)
        mk.sb(f"lgb{l}", [128, FC], F32)
        mk.sb(f"lbb{l}", [128, FC], F32)
        mk.view(f"b1b{l}", "hch", F32, 3840 + 44 * l, 44)
        mk.sb(f"b2b{l}", [128, FC], F32)

    mk.ps("psA"); mk.ps("psB"); mk.ps("psM0"); mk.ps("psM1")
    mk.ps("psU"); mk.ps("psV"); mk.ps("psW")
    mk.ps("psT", (128, 1024), BF16)

    mk.dma("sync", mk.t["x_fm"][:, :], x_in[:, :], writes=(mk.B("x_fm"),), key="init")
    for dst, src in (("bal", W["ba_al"]), ("bbe", W["ba_be"]), ("bvv", W["ba_v"]),
                     ("bbq_t", W["bb_q"]), ("bbk_t", W["bb_k"]),
                     ("c0_t", c0_in)):
        mk.dma("sync", mk.t[dst][:, :], src[:, :], writes=(mk.B(dst),), key="init")
    W["gc"] = gc_in
    W["dm"] = dm_in
    mk.op("vector", lambda e: e.memset(mk.t["ones_bf"][:, :], 1.0), writes=(mk.B("ones_bf"),))
    mk.op("vector", lambda e: e.memset(mk.t["eps"][:, :], 1e-5), writes=(mk.B("eps"),))
    mk.op("vector", lambda e: e.memset(mk.t["eps2"][:, :], 1e-24), writes=(mk.B("eps2"),))
    mk.op("vector", lambda e: e.memset(mk.t["zb"][:, :], 0.0), writes=(mk.B("zb"),))
    mk.op("vector", lambda e: e.memset(mk.t["onesr"][:, :], 1.0), writes=(mk.B("onesr"),))
    eyev = mk.nc.inline_tensor(np.eye(128, dtype=ml_dtypes.bfloat16), "eye128").ap()
    mk.dma("sync", mk.t["eye_t"][:, :], eyev[:, :], writes=(mk.B("eye_t"),), key="init")
    for l in range(2):
        for dst, src in ((f"qskb{l}", W[f"qsk{l}"]), (f"lgb{l}", W[f"lg{l}"]),
                         (f"lbb{l}", W[f"lb{l}"]), (f"b2b{l}", W[f"b2{l}"])):
            mk.dma("sync", mk.t[dst][:, :], src[:, :], writes=(mk.B(dst),), key="init")

    if "f" in parts:
        stage_weights(mk, W)
    if "a" in parts:
        build_attn_a(mk, W)
        mk.barrier()
    if "f" in parts:
        build_ffn(mk, W, 0)
        mk.barrier()
    if "b" in parts:
        build_attn_b(mk, W)
        mk.barrier()
    if "f" in parts:
        build_ffn(mk, W, 1)
        mk.barrier()

    mk.dma("sync", out_x[:, :], mk.t["x_fm"][:, :], reads=(mk.B("x_fm"),), key="outx")
    final_deps = [op for op in mk.f.ops if op.kind in ("dma", "switch", "cc")]
    mk.f.emit("sync", lambda e: e.nop(), deps=tuple(final_deps), kind="op")

    mk.f.lower(nc, stack)
    stack.close()
    return mk


def gamma_tiles():
    lg = np.log(GAMMA)
    j = np.arange(128)
    gk = np.zeros((128, 4 * NCH), np.float32)
    for c in range(NCH):
        for jc in range(4):
            gk[:, c * 4 + jc] = np.exp(-(c * CT + jc * 128 + j) * lg)
    i = np.arange(T)
    gq_t = np.broadcast_to(np.exp(i * lg), (128, T)).astype(ml_dtypes.bfloat16)
    ic = np.arange(2 * CT)
    gc_t = np.broadcast_to(np.exp((ic + T) * lg), (128, 2 * CT)).astype(ml_dtypes.bfloat16)
    dm = np.zeros((128, 4 * CT), np.float64)
    for jc in range(4):
        jj = jc * 128 + j[:, None]
        ii = np.arange(CT)[None, :]
        dm[:, jc * CT:(jc + 1) * CT] = np.where(ii >= jj, np.exp((ii - jj) * lg), 0.0)
    return gk, np.ascontiguousarray(gq_t), np.ascontiguousarray(gc_t), \
        dm.astype(ml_dtypes.bfloat16)


def host_inputs(inp, core):
    b, half = core // 2, core % 2
    x = np.asarray(inp["L0_attn_a_x"])
    tsl = slice(half * T, (half + 1) * T)
    xs = np.ascontiguousarray(x[b, tsl, :].T)
    x_fm = np.ascontiguousarray(xs.reshape(FC, 128, T).transpose(1, 0, 2).reshape(128, FC * T))
    gk, gq_t, gc_t, dm = gamma_tiles()
    d = {
        "x": x_fm.astype(np.float32),
        "wa_in": pack_w(np.asarray(inp["L0_attn_a_W_in"]).T),
        "wa_al": pack_w(np.asarray(inp["L0_attn_a_W_alpha"]).T),
        "wa_be": pack_w(np.asarray(inp["L0_attn_a_W_beta"]).T),
        "wa_v": pack_w(np.asarray(inp["L0_attn_a_W_v"]).T),
        "wa_out": pack_w(np.asarray(inp["L0_attn_a_W_out"]).T),
        "ba_al": pack_bias(inp["L0_attn_a_b_alpha"]),
        "ba_be": pack_bias(inp["L0_attn_a_b_beta"]),
        "ba_v": pack_bias(inp["L0_attn_a_b_v"]),
        "wb_q": pack_w(np.asarray(inp["L0_attn_b_Wq"]).T),
        "wb_k": pack_w(np.asarray(inp["L0_attn_b_Wk"]).T),
        "wb_v": pack_w_mov(np.asarray(inp["L0_attn_b_Wv"]).T),
        "bb_q": pack_bias(inp["L0_attn_b_bq"]),
        "bb_k": pack_bias(inp["L0_attn_b_bk"]),
        "c0": np.full((128, 1), float(half), np.float32),
        "gk": gk,
        "gq": gq_t,
        "gc": gc_t if half == 1 else np.zeros_like(gc_t),
        "dmask": dm,
        "rot": rot_tiles(half),
        "w0m": (np.full((1, CT), -1e30) if half == 0 else np.zeros((1, CT))
                ).astype(ml_dtypes.bfloat16),
    }
    perm = np.array([h * 64 + (dd + 32) % 64 for h in range(8) for dd in range(64)])
    perm64 = np.array([(dd + 32) % 64 for dd in range(64)])
    wqkv = np.asarray(inp["L0_ffn_wqkv"], np.float32)
    wo = np.asarray(inp["L0_ffn_wo"], np.float32)
    w1 = np.asarray(inp["L0_ffn_w1"], np.float32)
    w2 = np.asarray(inp["L0_ffn_w2"], np.float32)
    b1 = np.asarray(inp["L0_ffn_b1"], np.float32)
    b2 = np.asarray(inp["L0_ffn_b2"], np.float32)
    qs = np.asarray(inp["L0_ffn_q_scale"], np.float32)
    ks = np.asarray(inp["L0_ffn_k_scale"], np.float32)
    lg = np.asarray(inp["L0_ffn_ln_g"], np.float32)
    lb = np.asarray(inp["L0_ffn_ln_b"], np.float32)
    for l in range(2):
        wq = wqkv[l][:, 0:512]
        wk = wqkv[l][:, 512:1024]
        wv = wqkv[l][:, 1024:1536]
        d[f"wf_qkv{l}"] = pack_w(np.concatenate(
            [wq, wq[:, perm], wk, wk[:, perm]], axis=1))
        d[f"wf_vm{l}"] = pack_w_mov(wv)
        d[f"wf_o{l}"] = pack_w(wo[l])
        a_, g_ = w1[l][:, 0:2730], w1[l][:, 2730:5460]
        w1p = np.concatenate([np.pad(a_, ((0, 0), (0, 86))),
                              np.pad(g_, ((0, 0), (0, 86)))], axis=1)
        d[f"wf_1{l}"] = pack_w(w1p)
        d[f"wf_2{l}"] = pack_w_mov(np.pad(w2[l], ((0, 86), (0, 0))))
        b1a = np.pad(b1[l][0:2730], (0, 86))
        b1g = np.pad(b1[l][2730:5460], (0, 86))
        d[f"b1{l}"] = pack_bias(np.concatenate([b1a, b1g]))
        d[f"b2{l}"] = pack_bias(b2[l])
        d[f"lg{l}"] = pack_bias(lg[l])
        d[f"lb{l}"] = pack_bias(lb[l])
        d[f"qsk{l}"] = np.stack([np.tile(qs[l], 2), np.tile(qs[l][perm64], 2),
                                 np.tile(ks[l], 2), np.tile(ks[l][perm64], 2)],
                                axis=1).astype(np.float32)
    return d


def rot_tiles(half):
    DIM_HEAD, WINDOW, QK = 64, 512, 8.0
    d128 = np.arange(128)
    d64 = d128 % 64
    h32 = d64 % 32
    inv = 1.0 / 10000.0 ** (2.0 * h32 / DIM_HEAD)
    t = np.arange(half * T, (half + 1) * T, dtype=np.float64)
    th = t[None, :] * inv[:, None]
    base = (2.0 * h32 + 0.4 * DIM_HEAD) / (1.4 * DIM_HEAD)
    power = (t - 2048.0) / (WINDOW // 2)
    sc = base[:, None] ** power[None, :]
    sign = np.where(d64 < 32, -1.0, 1.0)[:, None]
    Aq = np.cos(th) * sc * QK
    Bq = sign * np.sin(th) * sc * QK
    Ak = np.cos(th) / sc
    Bk = sign * np.sin(th) / sc
    out = np.stack([Aq, Bq, Ak, Bk], axis=1)
    return np.ascontiguousarray(out).astype(ml_dtypes.bfloat16)


def host_output(res_list):
    out = np.zeros((4, 4096, 1024), np.float32)
    for core in range(8):
        b, half = core // 2, core % 2
        o = np.asarray(res_list[core]["out"])
        o = o.reshape(128, FC, T).transpose(1, 0, 2).reshape(1024, T)
        out[b, half * T:(half + 1) * T, :] = o.T
    return out


_BUILT = None


def kernel(**inputs):
    global _BUILT
    from concourse.bass_utils import run_bass_kernel_spmd
    if _BUILT is None:
        _BUILT = build()
    mkk = _BUILT
    ins = [host_inputs(inputs, c) for c in range(8)]
    res = run_bass_kernel_spmd(mkk.nc, ins, core_ids=list(range(8)))
    return host_output(res.results)


# version 15
# speedup vs baseline: 1.0228x; 1.0228x over previous
import numpy as np
import ml_dtypes
import concourse.bass as bass
import concourse.mybir as mybir

F32 = mybir.dt.float32
BF16 = mybir.dt.bfloat16
AF = mybir.ActivationFunctionType
ALU = mybir.AluOpType
AX = mybir.AxisListType

ENGINES = ("sync", "scalar", "vector", "tensor", "gpsimd")
D = 1024
T = 2048
CT = 512
NCH = T // CT
FC = D // 128
GAMMA = 0.96875
PAIRS = [[0, 1], [2, 3], [4, 5], [6, 7]]


class Buf:
    __slots__ = ("name", "w", "r")
    def __init__(self, name):
        self.name, self.w, self.r = name, None, []


class Op:
    __slots__ = ("idx", "eng", "fn", "deps", "kind", "inc", "sem", "cum_after",
                 "needs_inc", "cc_sem", "key")
    def __init__(self, idx, eng, fn, kind):
        self.idx, self.eng, self.fn, self.kind = idx, eng, fn, kind
        self.deps = set()
        self.cc_sem = None
        self.key = None


class Flow:
    TOTAL_KEYS = ("init",)

    def __init__(self):
        self.ops = []
        self.per_eng = {e: [] for e in ENGINES}
        self.n_cc = 0

    def emit(self, eng, fn, reads=(), writes=(), deps=(), kind="op", key=None):
        op = Op(len(self.ops), eng, fn, kind)
        op.key = key
        for d in deps:
            if d is not None:
                op.deps.add(d)
        for b in reads:
            if b.w is not None:
                op.deps.add(b.w)
        for b in writes:
            if b.w is not None:
                op.deps.add(b.w)
            for rop in b.r:
                op.deps.add(rop)
        for b in reads:
            b.r.append(op)
        for b in writes:
            b.w = op
            b.r = []
        if kind == "cc":
            op.cc_sem = self.n_cc
            self.n_cc += 1
        self.ops.append(op)
        self.per_eng[eng].append(op)
        return op

    def lower(self, nc, stack):
        eng_sem = {e: stack.enter_context(nc.semaphore(f"es_{e}")) for e in ENGINES}
        keys = []
        for op in self.ops:
            if op.kind in ("dma", "switch"):
                assert op.key is not None, f"dma op {op.idx} missing sem key"
                if op.key not in keys:
                    keys.append(op.key)
        key_sem = {k: stack.enter_context(nc.semaphore(f"k_{k}")) for k in keys}
        cc_sems = [stack.enter_context(nc.semaphore(f"cc_{i}")) for i in range(self.n_cc)]
        used = set()
        for op in self.ops:
            for d in op.deps:
                used.add(d)
        cum = {}
        for op in self.ops:
            if op.kind in ("dma", "switch"):
                op.sem, op.inc = key_sem[op.key], 16
                op.needs_inc = True
            elif op.kind == "cc":
                op.sem, op.inc = cc_sems[op.cc_sem], 1
                op.needs_inc = True
            else:
                op.sem, op.inc = eng_sem[op.eng], 1
                op.needs_inc = op in used
            if op.needs_inc:
                k = op.sem.name
                cum[k] = cum.get(k, 0) + op.inc
                op.cum_after = cum[k]
            else:
                op.cum_after = None
        for op in self.ops:
            if op.kind in ("dma", "switch") and op.key.endswith("_init"):
                op.cum_after = cum[op.sem.name]
        totals = dict(cum)

        block = stack.enter_context(nc.Block())

        def make_runner(e):
            def run(eng):
                waited = {}
                for op in self.per_eng[e]:
                    wants = {}
                    for d in op.deps:
                        assert d.needs_inc and d.cum_after is not None, \
                            f"dep {d.idx} of {op.idx} not inc'd"
                        k = d.sem.name
                        wants[k] = max(wants.get(k, (0, None))[0], d.cum_after), d.sem
                    for k, (val, sem) in wants.items():
                        if waited.get(k, 0) < val:
                            eng.wait_ge(sem, val)
                            waited[k] = val
                    inc_args = (op.sem, op.inc) if op.needs_inc else None
                    if op.kind == "switch":
                        op.fn(eng, inc_args)
                    else:
                        fns = op.fn if isinstance(op.fn, (list, tuple)) else (op.fn,)
                        inst = None
                        for f in fns:
                            inst = f(eng)
                        if op.needs_inc:
                            inst.then_inc(op.sem, op.inc)
            return run

        for e in ENGINES:
            getattr(block, e)(make_runner(e))
        return eng_sem, key_sem, cc_sems, totals


def pack_w(WT):
    WT = np.asarray(WT)
    K, M = WT.shape
    H = WT.reshape(K // 128, 128, M // 128, 128).transpose(2, 1, 0, 3)
    return np.ascontiguousarray(H.reshape(M // 128, 128, K)).astype(np.float32)


def pack_w_mov(WT):
    WT = np.asarray(WT)
    K, M = WT.shape
    return np.ascontiguousarray(WT.reshape(K // 128, 128, M)).astype(np.float32)


def pack_bias(b):
    return np.ascontiguousarray(np.asarray(b).reshape(-1, 128).T).astype(np.float32)


class View:
    __slots__ = ("_ap",)
    def __init__(self, ap):
        self._ap = ap
    def ap(self):
        return self._ap
    def __getitem__(self, k):
        return self._ap[k]


class MK:
    def __init__(self, dbg_names=()):
        self.nc = bass.Bass("TRN2", target_bir_lowering=False, debug=False)
        self.f = Flow()
        self.inputs = {}
        self.outputs = {}
        self.t = {}
        self.bufs = {}
        self.dbg_names = set(dbg_names)
        self._dbgs = {}
        self._exit = None
        self.all_bufs = []

    def din(self, name, shape, dtype=F32):
        self.inputs[name] = (tuple(shape), dtype)
        return self.nc.dram_tensor(name, list(shape), dtype, kind="ExternalInput").ap()

    def dout(self, name, shape, dtype=F32):
        self.outputs[name] = (tuple(shape), dtype)
        return self.nc.dram_tensor(name, list(shape), dtype, kind="ExternalOutput").ap()

    def dram(self, name, shape, dtype=F32):
        return self.nc.dram_tensor(name, list(shape), dtype, kind="Internal").ap()

    def _reg(self, name, ap, buf):
        v = View(ap)
        self.t[name] = v
        if buf is None:
            buf = Buf(name)
            self.all_bufs.append(buf)
        self.bufs[name] = buf
        return v

    def sb(self, name, shape, dtype=F32):
        t = self.nc.alloc_sbuf_tensor(name, list(shape), dtype)
        return self._reg(name, t.ap(), None)

    def view(self, name, base, dtype, start, cols, share=None):
        bap = self.t[base].ap()
        ap = bap.bitcast(dtype)[:, start:start + cols]
        return self._reg(name, ap, self.bufs[share] if share else None)

    def ps(self, name, shape=(128, 512), dtype=F32):
        t = self._exit.enter_context(self.nc.psum_tensor(name, list(shape), dtype))
        return self._reg(name, t.ap(), None)

    def B(self, name):
        return self.bufs[name]

    def dma(self, eng, out_ap, in_ap, reads=(), writes=(), deps=(), key=None, slow=False):
        if key is None:
            key = writes[0].name if writes else "out"
        key = eng[:2] + "_" + key
        return self.f.emit(
            eng, lambda e, o=out_ap, i=in_ap, sl=slow: e.dma_start(
                out=o, in_=i, allow_slow_non_contiguous=sl),
            reads=reads, writes=writes, deps=deps, kind="dma", key=key)

    def op(self, eng, fn, reads=(), writes=(), deps=()):
        return self.f.emit(eng, fn, reads=reads, writes=writes, deps=deps, kind="op")

    def cc(self, fn, reads=(), writes=(), deps=()):
        return self.f.emit("gpsimd", fn, reads=reads, writes=writes, deps=deps, kind="cc")

    def switch(self, eng, fn, reads=(), writes=(), deps=(), key=None):
        return self.f.emit(eng, fn, reads=reads, writes=writes, deps=deps,
                           kind="switch", key=key or "sw")

    def barrier(self):
        deps = set()
        for b in self.all_bufs:
            if b.w is not None:
                deps.add(b.w)
            deps.update(b.r)
        bop = self.f.emit("vector", lambda e: e.engine_nop(), deps=tuple(deps), kind="op")
        for b in self.all_bufs:
            b.w = bop
            b.r = []
        return bop

    def dump_stream(self, name, src_ap, cols, c, dtype=F32, reads=(), n=NCH):
        if name not in self.dbg_names:
            return
        key = "dbgs_" + name
        if key not in self._dbgs:
            self._dbgs[key] = self.dout(key, [128, cols * n], dtype)
        self.dma("sync", self._dbgs[key][:, c * cols:(c + 1) * cols], src_ap, reads=reads,
                 key=f"dmp_{name}{c}")


def layernorm(mk, src, c, dst, src_seg=None, src_bufs=None, dst_sl=None,
              patches=(), affine=None, u_name="ln_u", xb_sl=None, sq_sl=None):
    xb, sq = mk.t["ln_xb"], mk.t["ln_sq"]
    m_rep, msq, r_rep, u = mk.t["ln_m"], mk.t["ln_msq"], mk.t["ln_r"], mk.t[u_name]
    ones_bf, eps_t = mk.t["ones_bf"], mk.t["eps"]
    psA, psB = mk.t["psA"], mk.t["psB"]
    t0 = c * CT
    if src_seg is None:
        x_t = mk.t[src]
        src_seg = lambda fc: [(0, CT, x_t[:, fc * T + t0: fc * T + t0 + CT])]
        src_bufs = (mk.B(src),)
    if dst_sl is None:
        xt_ = mk.t[dst]
        dst_sl = lambda fc: (xt_[:, fc * CT:(fc + 1) * CT], mk.B(dst))
    pat_bufs = tuple(b for _, _, b in patches)
    if xb_sl is None:
        xb_sl = lambda fc: (xb[:, fc * CT:(fc + 1) * CT], mk.B("ln_xb"))
    if sq_sl is None:
        sq_sl = lambda fc: (sq[:, fc * CT:(fc + 1) * CT], mk.B("ln_sq"))
    for fc in range(FC):
        xa, xab = xb_sl(fc)
        sqa, sqb = sq_sl(fc)
        fns = []
        for off, wdt, sap in src_seg(fc):
            fns.append(lambda e, s=sap, d=xa[:, off: off + wdt]:
                       e.tensor_copy(d, s))
        for pfc, cap, _b in patches:
            if pfc == fc:
                fns.append(lambda e, s=cap, d=xa[:, 0:1]:
                           e.tensor_copy(d, s))
        mk.op("vector", fns, reads=tuple(src_bufs) + pat_bufs, writes=(xab,))
        mk.op("scalar", lambda e, o_=sqa, i_=xa: e.activation(
            out=o_[:, 0:CT], in_=i_[:, 0:CT], func=AF.Square),
            reads=(xab,), writes=(sqb,))
        mk.op("tensor", lambda e, k=fc, i_=xa: e.matmul(
            psA[:, :], ones_bf[:, :], i_[:, 0:CT],
            start=(k == 0), stop=(k == FC - 1)),
            reads=(xab, mk.B("ones_bf")), writes=(mk.B("psA"),))
        mk.op("tensor", lambda e, k=fc, i_=sqa: e.matmul(
            psB[:, :], ones_bf[:, :], i_[:, 0:CT],
            start=(k == 0), stop=(k == FC - 1)),
            reads=(sqb, mk.B("ones_bf")), writes=(mk.B("psB"),))
    mk.op("vector", lambda e: e.tensor_scalar_mul(m_rep[:, :], psA[:, :], 1.0 / D),
          reads=(mk.B("psA"),), writes=(mk.B("ln_m"),))
    mk.op("vector", lambda e: e.tensor_mul(msq[:, :], m_rep[:, :], m_rep[:, :]),
          reads=(mk.B("ln_m"),), writes=(mk.B("ln_msq"),))
    mk.op("vector", lambda e: e.scalar_tensor_tensor(msq[:, :], psB[:, :], 1.0 / D,
                                                     msq[:, :], ALU.mult, ALU.subtract),
          reads=(mk.B("psB"), mk.B("ln_msq")), writes=(mk.B("ln_msq"),))
    mk.op("scalar", lambda e: e.activation(out=r_rep[:, :], in_=msq[:, :], func=AF.Ln,
                                           bias=eps_t[:, 0:1]),
          reads=(mk.B("ln_msq"), mk.B("eps")), writes=(mk.B("ln_r"),))
    mk.op("scalar", lambda e: e.activation(out=r_rep[:, :], in_=r_rep[:, :], func=AF.Exp,
                                           scale=-0.5),
          reads=(mk.B("ln_r"),), writes=(mk.B("ln_r"),))
    fns = []
    for fc in range(FC):
        for off, wdt, sap in src_seg(fc):
            fns.append(lambda e, s=sap, d=u[:, fc * CT + off: fc * CT + off + wdt],
                       o=off, w_=wdt: e.tensor_sub(d, s, m_rep[:, o:o + w_]))
    for fc, cap, _b in patches:
        fns.append(lambda e, s=cap, d=u[:, fc * CT: fc * CT + 1]:
                   e.tensor_sub(d, s, m_rep[:, 0:1]))
    mk.op("vector", fns, reads=tuple(src_bufs) + pat_bufs + (mk.B("ln_m"),),
          writes=(mk.B(u_name),))
    fns = []
    dbufs = set()
    for fc in range(FC):
        dap, dbuf = dst_sl(fc)
        dbufs.add(dbuf)
        fns.append(lambda e, d=dap, k=fc: e.tensor_mul(
            d, u[:, k * CT:(k + 1) * CT], r_rep[:, :]))
    mk.op("vector", fns, reads=(mk.B(u_name), mk.B("ln_r")), writes=tuple(dbufs))
    if affine is not None:
        lg, lb, abufs = affine
        fns = []
        for fc in range(FC):
            dap, dbuf = dst_sl(fc)
            fns.append(lambda e, d=dap, k=fc: e.tensor_scalar(
                d, d, lg[:, k:k + 1], lb[:, k:k + 1], ALU.mult, ALU.add))
        mk.op("vector", fns, reads=tuple(dbufs) + tuple(abufs), writes=tuple(dbufs))


def proj(mk, w_hbm, src, out_cb, n_mc=FC, kc_n=FC, rhs_cols=CT, ld_eng="gpsimd",
         w_buf=None, ps_names=("psM0", "psM1"), w_names=None, w_rbuf=None):
    xt = mk.t[src]
    Bxt = mk.B(src)
    if w_names is None:
        w_names = ((w_buf or "w") + "0", (w_buf or "w") + "1")
    for mc in range(n_mc):
        wn = w_names[mc % len(w_names)]
        wb = mk.t[wn]
        Bwb = mk.B(wn)
        mk.dma(ld_eng, wb[:, 0:kc_n * 128], w_hbm[mc, :, :], writes=(Bwb,), key=wn,
               reads=(() if w_rbuf is None else (w_rbuf,)))
        psn = ps_names[mc % len(ps_names)]
        ps = mk.t[psn]
        Bps = mk.B(psn)
        mk.op("tensor",
              [(lambda e, k=kc, p=ps, w=wb: e.matmul(
                  p[:, 0:rhs_cols], w[:, k * 128:(k + 1) * 128],
                  xt[:, k * rhs_cols:(k + 1) * rhs_cols],
                  start=(k == 0), stop=(k == kc_n - 1))) for kc in range(kc_n)],
              reads=(Bwb, Bxt), writes=(Bps,))
        out_cb(mc, ps, Bps)


def pair_exchange(mk, in_dram_ap, out_dram_ap):
    return mk.cc(lambda e: e.collective_compute(
        "AllGather", ALU.bypass, replica_groups=PAIRS,
        ins=[in_dram_ap.tensor.ap().opt()], outs=[out_dram_ap.tensor.ap().opt()]))


def build_attn_a(mk, W):
    x_t, xt, xp = mk.t["x_fm"], mk.t["xt"], mk.t["xp"]
    Vb, Bb, hch = mk.t["Vb"], mk.t["Bb"], mk.t["hch"]
    hfull, pfull, tmpS = mk.t["hfull"], mk.t["pfull"], mk.t["tmpS"]
    hcar, pcar, hin, hinm = mk.t["hcar"], mk.t["pcar"], mk.t["hin"], mk.t["hinm"]
    zb, c0_t = mk.t["zb"], mk.t["c0_t"]
    bal, bbe, bvv = mk.t["bal"], mk.t["bbe"], mk.t["bvv"]
    Bx = mk.B("x_fm")

    mk.op("vector", lambda e: e.memset(pcar[:, :], 1.0), writes=(mk.B("pcar"),))
    mk.op("vector", lambda e: e.memset(hcar[:, :], 0.0), writes=(mk.B("hcar"),))

    for c in range(NCH):
        layernorm(mk, "x_fm", c, "xt")
        def ev_xp(mc, ps, Bps):
            mk.op("scalar", lambda e, m=mc, p=ps: e.activation(
                out=xp[:, m * CT:(m + 1) * CT], in_=p[:, 0:CT], func=AF.Silu),
                reads=(Bps,), writes=(mk.B("xp"),))
        proj(mk, mk.t["wst_ain"].ap(), "xt", ev_xp, ld_eng="sync",
             w_rbuf=mk.B("wst_ain"), ps_names=("psM0", "psM1", "psU", "psV"))
        def ev_v(mc, ps, Bps):
            mk.op("scalar", lambda e, m=mc, p=ps: e.activation(
                out=Vb[:, m * CT:(m + 1) * CT], in_=p[:, 0:CT], func=AF.Tanh,
                bias=bvv[:, m:m + 1]),
                reads=(Bps, mk.B("bvv")), writes=(mk.B("Vb"),))
        proj(mk, mk.t["wst_av"].ap(), "xp", ev_v, ld_eng="sync",
             w_rbuf=mk.B("wst_av"), ps_names=("psM0", "psM1", "psU", "psV"))
        def ev_b(mc, ps, Bps):
            mk.op("scalar", lambda e, m=mc, p=ps: e.activation(
                out=Bb[:, m * CT:(m + 1) * CT], in_=p[:, 0:CT], func=AF.Sigmoid,
                bias=bbe[:, m:m + 1]),
                reads=(Bps, mk.B("bbe")), writes=(mk.B("Bb"),))
        proj(mk, mk.t["wst_abe"].ap(), "xp", ev_b, ld_eng="sync",
             w_rbuf=mk.B("wst_abe"), ps_names=("psM0", "psM1", "psU", "psV"))
        mk.op("vector",
              [(lambda e, k=fc: e.tensor_mul(
                  Vb[:, k * CT:(k + 1) * CT], Vb[:, k * CT:(k + 1) * CT],
                  Bb[:, k * CT:(k + 1) * CT])) for fc in range(FC)],
              reads=(mk.B("Vb"), mk.B("Bb")), writes=(mk.B("Vb"),))
        def ev_a(mc, ps, Bps):
            mk.op("scalar", lambda e, m=mc, p=ps: e.activation(
                out=Bb[:, m * CT:(m + 1) * CT], in_=p[:, 0:CT], func=AF.Sigmoid,
                bias=bal[:, m:m + 1]),
                reads=(Bps, mk.B("bal")), writes=(mk.B("Bb"),))
        proj(mk, mk.t["wst_aal"].ap(), "xp", ev_a, ld_eng="sync",
             w_rbuf=mk.B("wst_aal"), ps_names=("psM0", "psM1", "psU", "psV"))
        mk.op("vector",
              [(lambda e, k=fc: e.tensor_tensor_scan(
                  hch[:, k * CT:(k + 1) * CT], Bb[:, k * CT:(k + 1) * CT],
                  Vb[:, k * CT:(k + 1) * CT], hcar[:, k:k + 1], ALU.mult, ALU.add))
               for fc in range(FC)],
              reads=(mk.B("Bb"), mk.B("Vb"), mk.B("hcar")), writes=(mk.B("hch"),))
        mk.op("vector", lambda e: e.tensor_copy(hcar[:, 0:FC], hch[:, CT - 1::CT]),
              reads=(mk.B("hch"),), writes=(mk.B("hcar"),))
        mk.op("scalar",
              [(lambda e, k=fc, cc_=c: e.activation(
                  out=hfull[:, k * T + cc_ * CT: k * T + cc_ * CT + CT],
                  in_=hch[:, k * CT:(k + 1) * CT], func=AF.Identity)) for fc in range(FC)],
              reads=(mk.B("hch"),), writes=(mk.B("hfull"),))
        mk.op("vector",
              [(lambda e, k=fc: e.tensor_tensor_scan(
                  Vb[:, k * CT:(k + 1) * CT], Bb[:, k * CT:(k + 1) * CT],
                  zb[:, 0:CT], pcar[:, k:k + 1], ALU.mult, ALU.add)) for fc in range(FC)],
              reads=(mk.B("Bb"), mk.B("zb"), mk.B("pcar")), writes=(mk.B("Vb"),))
        mk.op("vector", lambda e: e.tensor_copy(pcar[:, 0:FC], Vb[:, CT - 1::CT]),
              reads=(mk.B("Vb"),), writes=(mk.B("pcar"),))
        mk.op("scalar",
              [(lambda e, k=fc, cc_=c: e.activation(
                  out=pfull[:, k * T + cc_ * CT: k * T + cc_ * CT + CT],
                  in_=Vb[:, k * CT:(k + 1) * CT], func=AF.Identity)) for fc in range(FC)],
              reads=(mk.B("Vb"),), writes=(mk.B("pfull"),))

    x1_in, x1_out = mk.t["x1_in"].ap(), mk.t["x1_out"].ap()
    bounce1 = mk.dma("gpsimd", x1_in[:, :], hcar[:, 0:FC], reads=(mk.B("hcar"),), key="xch1")
    cc1 = pair_exchange(mk, x1_in, x1_out)
    cc1.deps.add(bounce1)

    def x1_consume(eng, inc_args):
        for arm in eng.switch_core_id(n=128):
            if arm.logical % 2 == 1:
                eng.wait_ge(cc1.sem, 1)
                inst = eng.dma_start(out=hin[:, :], in_=x1_out[0:128, :])
            else:
                inst = eng.dma_start(out=hin[:, :], in_=x1_in[:, :])
            if inc_args:
                inst.then_inc(*inc_args)
    mk.switch("gpsimd", x1_consume, writes=(mk.B("hin"),), deps=(bounce1,), key="sw1")
    mk.op("vector", lambda e: e.tensor_scalar_mul(hinm[:, :], hin[:, :], c0_t[:, 0:1]),
          reads=(mk.B("hin"), mk.B("c0_t")), writes=(mk.B("hinm"),))

    for c in range(NCH):
        mk.op("vector",
              [(lambda e, k=fc, s=slice(fc * T + c * CT, fc * T + c * CT + CT):
                e.scalar_tensor_tensor(
                    tmpS[:, k * CT:(k + 1) * CT], pfull[:, s], hinm[:, k:k + 1],
                    hfull[:, s], ALU.mult, ALU.add)) for fc in range(FC)],
              reads=(mk.B("pfull"), mk.B("hfull"), mk.B("hinm")), writes=(mk.B("tmpS"),))
        mk.op("scalar", lambda e: e.activation(out=hch[:, :], in_=tmpS[:, :], func=AF.Silu),
              reads=(mk.B("tmpS"),), writes=(mk.B("hch"),))
        mk.op("vector",
              [(lambda e, k=fc: e.tensor_mul(
                  xt[:, k * CT:(k + 1) * CT], tmpS[:, k * CT:(k + 1) * CT],
                  hch[:, k * CT:(k + 1) * CT])) for fc in range(FC)],
              reads=(mk.B("tmpS"), mk.B("hch")), writes=(mk.B("xt"),))
        def ev_o(mc, ps, Bps, cc_=c):
            s = slice(mc * T + cc_ * CT, mc * T + cc_ * CT + CT)
            mk.op("vector", lambda e, ss=s, p=ps: e.tensor_add(
                x_t[:, ss], x_t[:, ss], p[:, 0:CT]),
                reads=(Bps, Bx), writes=(Bx,))
        proj(mk, mk.t["wst_aout"].ap(), "xt", ev_o, ld_eng="sync",
             w_rbuf=mk.B("wst_aout"), ps_names=("psM0", "psM1", "psU", "psV"))


def build_attn_b(mk, W):
    x_t = mk.t["x_fm"]
    Bx = mk.B("x_fm")
    xt = mk.t["xt"]
    q_fm, k_fm, vtm, ktm = mk.t["q_fm"], mk.t["k_fm"], mk.t["vtm"], mk.t["ktm"]
    ptr, qgs, qgc, A_sb = mk.t["ptr"], mk.t["qgs"], mk.t["qgc"], mk.t["A_sb"]
    wvm, gk_t, gq_t, gc_t = mk.t["wvm"], mk.t["gk_t"], mk.t["gq_t"], mk.t["gc_t"]
    dmask = mk.t["dmask_t"]
    bbq, bbk, eye = mk.t["bbq_t"], mk.t["bbk_t"], mk.t["eye_t"]
    psT = mk.t["psT"]
    Ain = [mk.t["Ain0"], mk.t["Ain1"]]

    mk.dma("gpsimd", wvm.ap().rearrange("p (k m) -> p k m", k=FC),
           W["wb_v"].transpose([1, 0, 2]), writes=(mk.B("wvm"),), key="wvm")
    mk.dma("gpsimd", gq_t[:, :], W["gq"][:, :], writes=(mk.B("gq_t"),), key="gqd")
    mk.dma("gpsimd", gk_t[:, :], W["gk"][:, :], writes=(mk.B("gk_t"),), key="gkd")
    mk.dma("gpsimd", dmask[:, :], W["dm"][:, :], writes=(mk.B("dmask_t"),), key="dmd")
    mk.dma("gpsimd", gc_t[:, :], W["gc"][:, :], writes=(mk.B("gc_t"),), key="gcd")
    mk.op("vector", lambda e: e.memset(A_sb[:, :], 0.0), writes=(mk.B("A_sb"),))

    for c in range(NCH):
        layernorm(mk, "x_fm", c, "xt")
        def ev_q(mc, ps, Bps):
            mk.op("scalar", lambda e, m=mc, p=ps: e.activation(
                out=q_fm[:, m * CT:(m + 1) * CT], in_=p[:, 0:CT], func=AF.Identity,
                bias=bbq[:, m:m + 1]),
                reads=(Bps, mk.B("bbq_t")), writes=(mk.B("q_fm"),))
        proj(mk, mk.t["wst_bq"].ap(), "xt", ev_q, ld_eng="sync",
             w_rbuf=mk.B("wst_bq"), w_names=("w0", "w1", "vm0", "vm1"),
             ps_names=("psM0", "psM1", "psU", "psV"))
        def ev_k(mc, ps, Bps):
            mk.op("scalar", lambda e, m=mc, p=ps: e.activation(
                out=k_fm[:, m * CT:(m + 1) * CT], in_=p[:, 0:CT], func=AF.Identity,
                bias=bbk[:, m:m + 1]),
                reads=(Bps, mk.B("bbk_t")), writes=(mk.B("k_fm"),))
        proj(mk, mk.t["wst_bk"].ap(), "xt", ev_k, ld_eng="sync",
             w_rbuf=mk.B("wst_bk"), w_names=("w0", "w1", "vm0", "vm1"),
             ps_names=("psM0", "psM1", "psU", "psV"))
        for jc in range(4):
            for vfn in range(2):
                psn = ["psM0", "psM1", "psU", "psV"][(jc * 2 + vfn) % 4]
                ps = mk.t[psn]
                mk.op("tensor",
                      [(lambda e, dc=dinc, p=ps, j=jc, v=vfn: e.matmul(
                          p[:, 0:CT], xt[:, dc * CT + j * 128: dc * CT + j * 128 + 128],
                          wvm[:, dc * D + v * CT: dc * D + v * CT + CT],
                          start=(dc == 0), stop=(dc == FC - 1))) for dinc in range(FC)],
                      reads=(mk.B("xt"), mk.B("wvm")), writes=(mk.B(psn),))
                mk.op("scalar",
                      lambda e, j=jc, v=vfn, p=ps: e.activation(
                          out=vtm[:, j * D + v * CT: j * D + v * CT + CT],
                          in_=p[:, 0:CT], func=AF.Identity),
                      reads=(mk.B(psn),), writes=(mk.B("vtm"),))
        for jc in range(4):
            mk.op("tensor",
                  [(lambda e, kf=kfc, j=jc: e.transpose(
                      psT[:, kf * 128:(kf + 1) * 128],
                      k_fm[:, kf * CT + j * 128: kf * CT + j * 128 + 128],
                      eye[:, :])) for kfc in range(FC)],
                  reads=(mk.B("k_fm"), mk.B("eye_t")), writes=(mk.B("psT"),))
            mk.op("vector",
                  lambda e, j=jc, cc_=c: e.tensor_scalar_mul(
                      ktm[:, j * D:(j + 1) * D], psT[:, 0:D],
                      gk_t[:, cc_ * 4 + j: cc_ * 4 + j + 1]),
                  reads=(mk.B("psT"), mk.B("gk_t")), writes=(mk.B("ktm"),))
        mk.op("vector",
              [(lambda e, k=fc, cc_=c: e.tensor_mul(
                  qgs[:, k * CT:(k + 1) * CT], q_fm[:, k * CT:(k + 1) * CT],
                  gq_t[:, cc_ * CT:(cc_ + 1) * CT])) for fc in range(FC)],
              reads=(mk.B("q_fm"), mk.B("gq_t")), writes=(mk.B("qgs"),))
        if c < 2:
            mk.op("vector",
                  [(lambda e, k=fc, cc_=c: e.tensor_mul(
                      qgc[:, k * 2 * CT + cc_ * CT: k * 2 * CT + cc_ * CT + CT],
                      q_fm[:, k * CT:(k + 1) * CT],
                      gc_t[:, cc_ * CT:(cc_ + 1) * CT])) for fc in range(FC)],
                  reads=(mk.B("q_fm"), mk.B("gc_t")), writes=(mk.B("qgc"),))
        for jc in range(4):
            psn = ["psM0", "psM1", "psU", "psV"][jc % 4]
            ps = mk.t[psn]
            mk.op("tensor",
                  [(lambda e, kf=kfc, p=ps, j=jc: e.matmul(
                      p[:, 0:CT], k_fm[:, kf * CT + j * 128: kf * CT + j * 128 + 128],
                      q_fm[:, kf * CT:(kf + 1) * CT],
                      start=(kf == 0), stop=(kf == FC - 1))) for kfc in range(FC)],
                  reads=(mk.B("k_fm"), mk.B("q_fm")), writes=(mk.B(psn),))
            mk.op("vector",
                  lambda e, j=jc, p=ps: e.tensor_mul(
                      ptr[:, j * CT:(j + 1) * CT], p[:, 0:CT],
                      dmask[:, j * CT:(j + 1) * CT]),
                  reads=(mk.B(psn), mk.B("dmask_t")), writes=(mk.B("ptr"),))
        for vfc in range(FC):
            psn = ["psM0", "psM1", "psU", "psV"][vfc % 4]
            ps = mk.t[psn]
            fns = []
            for jc in range(4):
                fns.append(lambda e, j=jc, v=vfc, p=ps: e.matmul(
                    p[:, 0:CT], vtm[:, j * D + v * 128: j * D + v * 128 + 128],
                    ptr[:, j * CT:(j + 1) * CT], start=(j == 0), stop=False))
            for kfc in range(FC):
                fns.append(lambda e, kf=kfc, v=vfc, p=ps: e.matmul(
                    p[:, 0:CT], A_sb[:, kf * D + v * 128: kf * D + v * 128 + 128],
                    qgs[:, kf * CT:(kf + 1) * CT], start=False, stop=(kf == FC - 1)))
            mk.op("tensor", fns,
                  reads=(mk.B("vtm"), mk.B("ptr"), mk.B("A_sb"), mk.B("qgs")),
                  writes=(mk.B(psn),))
            s = slice(vfc * T + c * CT, vfc * T + c * CT + CT)
            mk.op("vector", lambda e, ss=s, p=ps: e.tensor_add(x_t[:, ss], x_t[:, ss],
                                                               p[:, 0:CT]),
                  reads=(mk.B(psn), Bx), writes=(Bx,))
        for kfc in range(FC):
            for vfn in range(2):
                psn = "psA" if vfn == 0 else "psB"
                ps = mk.t[psn]
                mk.op("tensor",
                      [(lambda e, j=jc, p=ps, kf=kfc, v=vfn: e.matmul(
                          p[:, 0:CT], ktm[:, j * D + kf * 128: j * D + kf * 128 + 128],
                          vtm[:, j * D + v * CT: j * D + v * CT + CT],
                          start=(j == 0), stop=(j == 3))) for jc in range(4)],
                      reads=(mk.B("ktm"), mk.B("vtm")), writes=(mk.B(psn),))
                s = slice(kfc * D + vfn * CT, kfc * D + vfn * CT + CT)
                mk.op("vector", lambda e, ss=s, p=ps: e.tensor_add(
                    A_sb[:, ss], A_sb[:, ss], p[:, 0:CT]),
                    reads=(mk.B(psn), mk.B("A_sb")), writes=(mk.B("A_sb"),))
        mk.dump_stream("bq", q_fm.ap()[:, :], FC * CT, c, BF16, reads=(mk.B("q_fm"),))
        mk.dump_stream("bptr", ptr.ap()[:, :], 4 * CT, c, BF16, reads=(mk.B("ptr"),))
        mk.dump_stream("bA", A_sb.ap()[:, :], FC * D, c, BF16, reads=(mk.B("A_sb"),))

    x8_in, x8_out, x8_sel = mk.t["x8_in"].ap(), mk.t["x8_out"].ap(), mk.t["x8_sel"].ap()
    b8 = mk.dma("gpsimd", x8_in[:, :], A_sb[:, :], reads=(mk.B("A_sb"),), key="xch8")
    cc8 = pair_exchange(mk, x8_in, x8_out)
    cc8.deps.add(b8)

    def x8_consume(eng, inc_args):
        for arm in eng.switch_core_id(n=128):
            if arm.logical % 2 == 1:
                eng.wait_ge(cc8.sem, 1)
                inst = eng.dma_start(out=x8_sel[:, :], in_=x8_out[0:128, :])
            else:
                inst = eng.dma_start(out=x8_sel[:, :], in_=x8_in[:, :])
            if inc_args:
                inst.then_inc(*inc_args)
    mk.switch("gpsimd", x8_consume, writes=(mk.B("x8_sel"),), deps=(b8,), key="sw8")

    for c in range(2):
        for half in range(2):
            for kfc in range(FC):
                an = f"Ain{kfc % 2}"
                a_t = Ain[kfc % 2]
                mk.dma("gpsimd", a_t[:, :], x8_sel[:, kfc * D:(kfc + 1) * D],
                       reads=(mk.B("x8_sel"),), writes=(mk.B(an),), key=an)
                for vi in range(4):
                    vfc = half * 4 + vi
                    psn = ["psM0", "psM1", "psA", "psB"][vi]
                    ps = mk.t[psn]
                    mk.op("tensor",
                          lambda e, kf=kfc, v=vfc, p=ps, at=a_t, cc_=c: e.matmul(
                              p[:, 0:CT], at[:, v * 128:(v + 1) * 128],
                              qgc[:, kf * 2 * CT + cc_ * CT: kf * 2 * CT + cc_ * CT + CT],
                              start=(kf == 0), stop=(kf == FC - 1)),
                          reads=(mk.B(an), mk.B("qgc")), writes=(mk.B(psn),))
            for vi in range(4):
                vfc = half * 4 + vi
                psn = ["psM0", "psM1", "psA", "psB"][vi]
                ps = mk.t[psn]
                s = slice(vfc * T + c * CT, vfc * T + c * CT + CT)
                mk.op("vector", lambda e, ss=s, p=ps: e.tensor_add(
                    x_t[:, ss], x_t[:, ss], p[:, 0:CT]),
                    reads=(mk.B(psn), Bx), writes=(Bx,))



def xf_ap(mk, fc):
    return (mk.t["xf03"], (fc % 4) * T) if fc < 4 else (mk.t["xf47"], (fc % 4) * T)


def xf_buf(mk, fc):
    return mk.B("xf03") if fc < 4 else mk.B("xf47")


def stage_weights(mk, W):
    jobs = []
    for nm, wn in (("wa_in", "ain"), ("wa_v", "av"), ("wa_be", "abe"),
                   ("wa_al", "aal"), ("wa_out", "aout"),
                   ("wb_q", "bq"), ("wb_k", "bk")):
        jobs.append((W[nm], mk.t[f"wst_{wn}"].ap(), 8, 1024))
    for l in range(2):
        jobs.append((W[f"wf_qkv{l}"], mk.t[f"wst_qkv{l}"].ap(), 16, 1024))
        jobs.append((W[f"wf_vm{l}"], mk.t[f"wst_vm{l}"].ap(), 8, 512))
        jobs.append((W[f"wf_o{l}"], mk.t[f"wst_o{l}"].ap(), 8, 512))
        jobs.append((W[f"wf_1{l}"], mk.t[f"wst_1{l}"].ap(), 44, 1024))
        jobs.append((W[f"wf_2{l}"], mk.t[f"wst_2{l}"].ap(), 22, 1024))
    for src, dst, n, cols in jobs:
        nm = dst.tensor.name
        dbuf = mk.B(nm)
        mk.dma("gpsimd", dst[:, :, :], src[:, :, :], writes=(dbuf,),
               key=f"st_{nm}")


def send_pair(mk, tag, src_ap, src_bufs, cols, dtype=F32, slow=False):
    in_d = mk.dram(f"xin_{tag}", [128, cols], dtype)
    out_d = mk.dram(f"xout_{tag}", [256, cols], dtype)
    b = mk.dma("gpsimd", in_d[:, :], src_ap, reads=tuple(src_bufs), key=f"xb_{tag}",
               slow=slow)
    cc = pair_exchange(mk, in_d, out_d)
    cc.deps.add(b)
    return cc, in_d, out_d, b


def recv_pair(mk, tag, cc, in_d, out_d, dst_ap, dst_bufs, dep):
    def consume(eng, inc_args):
        for arm in eng.switch_core_id(n=128):
            if arm.logical % 2 == 1:
                eng.wait_ge(cc.sem, 1)
                inst = eng.dma_start(out=dst_ap, in_=out_d[0:128, :])
            else:
                inst = eng.dma_start(out=dst_ap, in_=in_d[:, :])
            if inc_args:
                inst.then_inc(*inc_args)
    return mk.switch("gpsimd", consume, writes=tuple(dst_bufs), deps=(dep,),
                     key=f"sw_{tag}")


def build_ffn(mk, W, ci):
    x_t = mk.t["x_fm"]
    Bx = mk.B("x_fm")
    qin = mk.t["f_qin"]
    fq, fqs, fk, fks = mk.t["f_q"], mk.t["f_qs"], mk.t["f_k"], mk.t["f_ks"]
    fP, fsq = mk.t["f_P"], mk.t["f_sqv"]
    kw = [mk.t["f_ka"], mk.t["f_kb"]]
    vw = [mk.t["f_va"], mk.t["f_vb"]]
    k0, v0, fPT = mk.t["f_k0"], mk.t["f_v0"], mk.t["f_PT"]
    fo, frc = mk.t["f_o"], mk.t["f_rc"]
    nrm, cols = mk.t["psW"], mk.t["f_cols"]
    psT, psU, psV = mk.t["psT"], mk.t["psU"], mk.t["psV"]
    c0_t = mk.t["c0_t"]
    shb1, shb2, sb3 = mk.t["shb1"], mk.t["shb2"], mk.t["sb3"]
    eye, eps2 = mk.t["eye_t"], mk.t["eps2"]
    w0m, onesr = mk.t["w0m_t"], mk.t["onesr"]
    vm = [mk.t["vm0"], mk.t["vm1"]]
    gel = [mk.t["gel0"], mk.t["gel1"]]
    t47 = mk.t["xf47"]

    mk.dma("sync", w0m[0:1, :], W["w0m"][:, :], writes=(mk.B("w0m_t"),), key="w0md")
    hdv = mk.nc.inline_tensor(
        np.kron(np.eye(2), np.ones((64, 64))).astype(ml_dtypes.bfloat16), f"hd128_{ci}").ap()
    mk.dma("sync", mk.t["hdones"][:, :], hdv[:, :], writes=(mk.B("hdones"),), key="hdd")
    _p = np.arange(128)[:, None]
    _c = np.arange(128)[None, :]
    mtlv = mk.nc.inline_tensor(
        np.where(_c < _p, -1e30, 0.0).astype(ml_dtypes.bfloat16), f"mtl_{ci}").ap()
    mtrv = mk.nc.inline_tensor(
        np.where(_c > _p, -1e30, 0.0).astype(ml_dtypes.bfloat16), f"mtr_{ci}").ap()
    mk.dma("sync", mk.t["mtl"][:, :], mtlv[:, :], writes=(mk.B("mtl"),), key="mtld")
    mk.dma("sync", mk.t["mtr"][:, :], mtrv[:, :], writes=(mk.B("mtr"),), key="mtrd")
    for c in range(NCH):
        def dst_sl(fc, c_=c):
            t_, base = xf_ap(mk, fc)
            return t_[:, base + c_ * CT: base + c_ * CT + CT], xf_buf(mk, fc)
        layernorm(mk, "x_fm", c, None, dst_sl=dst_sl)

    def send_shift(tag):
        return send_pair(mk, tag, t47.ap()[:, T - 1::T], (mk.B("xf47"),), 4, slow=True)

    def dump_xf(tag):
        if ci != 0:
            return
        mk.dump_stream(tag + "a", mk.t["xf03"].ap()[:, :], 4 * T, 0, F32,
                       reads=(mk.B("xf03"),), n=1)
        mk.dump_stream(tag + "b", mk.t["xf47"].ap()[:, :], 4 * T, 0, F32,
                       reads=(mk.B("xf47"),), n=1)
    dump_xf("ff0")

    cc_s1, in_s1, out_s1, b_s1 = send_shift(f"s1c{ci}l0")

    for li in range(2):
        lw = {k: mk.t[f"wst_{k}{li}"].ap() for k in ("qkv", "vm", "o", "1", "2")}
        qskb = mk.t[f"qskb{li}"]
        lgp, lbp = mk.t[f"lgb{li}"], mk.t[f"lbb{li}"]
        b1p, b2p = mk.t[f"b1b{li}"], mk.t[f"b2b{li}"]

        if li == 1:
            cc_s1, in_s1, out_s1, b_s1 = send_shift(f"s1c{ci}l1")
        recv_pair(mk, f"s1c{ci}l{li}", cc_s1, in_s1, out_s1, shb1[:, 0:4],
                  (mk.B("shb1"),), b_s1)
        mk.op("vector", lambda e: e.tensor_scalar_mul(shb1[:, 4:8], shb1[:, 0:4],
                                                      c0_t[:, 0:1]),
              reads=(mk.B("shb1"), mk.B("c0_t")), writes=(mk.B("shb1"),))

        def build_qin(w):
            fns = []
            rbufs = {mk.B("xf03"), mk.B("xf47")}
            for fc in range(FC):
                t_, base = xf_ap(mk, fc)
                off = w * CT
                if fc < 4:
                    fns.append(lambda e, s=t_[:, base + off: base + off + CT],
                               d=qin[:, fc * CT:(fc + 1) * CT]: e.tensor_copy(d, s))
                elif w > 0:
                    fns.append(lambda e, s=t_[:, base + off - 1: base + off - 1 + CT],
                               d=qin[:, fc * CT:(fc + 1) * CT]: e.tensor_copy(d, s))
                else:
                    fns.append(lambda e, s=t_[:, base: base + CT - 1],
                               d=qin[:, fc * CT + 1:(fc + 1) * CT]: e.tensor_copy(d, s))
                    fns.append(lambda e, s=shb1[:, fc: fc + 1],
                               d=qin[:, fc * CT: fc * CT + 1]: e.tensor_copy(d, s))
            if w == 0:
                rbufs.add(mk.B("shb1"))
            mk.op("vector", fns, reads=tuple(rbufs),
                  writes=(mk.B("f_qin"), mk.B("f_P0"), mk.B("f_P1"),
                          mk.B("f_P2"), mk.B("f_P3")))

        def qkv(w, kdst, vdst, kdb, vdb, skip_qin=False):
            if not skip_qin:
                build_qin(w)
            PSV = ("psM0", "psM1", "psU", "psV")
            VW = ("vm0", "vm1", "w0", "w1")
            for kc in range(FC):
                vmn = VW[kc % 4]
                vt = mk.t[vmn]
                mk.dma("sync", vt[:, 0:CT], lw["vm"][kc, :, :],
                       writes=(mk.B(vmn),), key=vmn)
                mk.op("tensor",
                      [(lambda e, k_=kc, j=jc_, vb_=vt: e.matmul(
                          mk.t[PSV[j]][:, 0:CT],
                          qin[:, k_ * CT + j * 128: k_ * CT + j * 128 + 128],
                          vb_[:, 0:CT], start=(k_ == 0), stop=(k_ == FC - 1)))
                       for jc_ in range(4)],
                      reads=(mk.B(vmn), mk.B("f_qin")),
                      writes=tuple(mk.B(p) for p in PSV))
            for jc in range(4):
                mk.op("scalar", lambda e, j=jc, p=mk.t[PSV[jc]]: e.activation(
                    out=vdst[:, j * CT:(j + 1) * CT], in_=p[:, 0:CT], func=AF.Identity),
                    reads=(mk.B(PSV[jc]),), writes=(vdb,))
            def ev(mc, ps, Bps):
                tgt, tb = [(fq, "f_q"), (fqs, "f_qs"), (fk, "f_k"), (fks, "f_ks")][mc // 4]
                j = mc % 4
                mk.op("scalar", lambda e, p=ps, t_=tgt, jj=j: e.activation(
                    out=t_[:, jj * CT:(jj + 1) * CT], in_=p[:, 0:CT], func=AF.Identity),
                    reads=(Bps,), writes=(mk.B(tb),))
            proj(mk, lw["qkv"], "f_qin", ev, n_mc=16, ld_eng="sync",
                 w_names=("w0", "w1", "vm0", "vm1"),
                 ps_names=("psM0", "psM1", "psU", "psV"))
            if ci == 0 and li == 0:
                mk.dump_stream("fqraw", fq.ap()[:, :], 4 * CT, w, BF16, reads=(mk.B("f_q"),))
                mk.dump_stream("fqsraw", fqs.ap()[:, :], 4 * CT, w, BF16, reads=(mk.B("f_qs"),))
            mk.dma("sync", frc.ap().rearrange("p (a c) -> p a c", a=4),
                   W["rot"][:, :, w * CT:(w + 1) * CT], writes=(mk.B("f_rc"),), key="frc")
            mk.op("vector",
                  [(lambda e, i_=i, qb=qskb: e.tensor_scalar_mul(
                      frc[:, i_ * CT:(i_ + 1) * CT], frc[:, i_ * CT:(i_ + 1) * CT],
                      qb[:, i_:i_ + 1])) for i in range(4)],
                  reads=(mk.B("f_rc"), mk.B(f"qskb{li}")), writes=(mk.B("f_rc"),))
            PRL = ((fq, fqs, "f_q", "f_qs", "f_sqv", ("psM0", "psM1"), "f_rr0"),
                   (fk, fks, "f_k", "f_ks", "f_sqv2", ("psU", "psV"), "f_rr1"))

            def l2_muls(j, pi):
                a_, b_, an, bn, _sq, _ps, rrn = PRL[pi]
                rslot = mk.t[rrn]
                mk.op("vector",
                      [(lambda e, j_=j, t=a_: e.tensor_mul(
                          t[:, j_ * CT:(j_ + 1) * CT], t[:, j_ * CT:(j_ + 1) * CT],
                          rslot[:, :])),
                       (lambda e, j_=j, t=b_: e.tensor_mul(
                          t[:, j_ * CT:(j_ + 1) * CT], t[:, j_ * CT:(j_ + 1) * CT],
                          rslot[:, :]))],
                      reads=(mk.B(an), mk.B(bn), mk.B(rrn)),
                      writes=(mk.B(an), mk.B(bn)))

            for j_ in range(4):
                for pi_ in range(2):
                    a_, b_, an, bn, sqn, psns, rrn = PRL[pi_]
                    sqt = mk.t[sqn]
                    psnn = psns[j_ % 2]
                    psn_t = mk.t[psnn]
                    mk.op("vector", lambda e, j=j_, t=a_, sq_=sqt: e.tensor_mul(
                        sq_[:, 0:CT], t[:, j * CT:(j + 1) * CT],
                        t[:, j * CT:(j + 1) * CT]),
                        reads=(mk.B(an),), writes=(mk.B(sqn),))
                    mk.op("tensor", lambda e, p=psn_t, sq_=sqt: e.matmul(
                        p[:, 0:CT], mk.t["hdones"][:, :], sq_[:, 0:CT],
                        start=True, stop=True),
                        reads=(mk.B(sqn), mk.B("hdones")), writes=(mk.B(psnn),))
                if j_ > 0:
                    l2_muls(j_ - 1, 0)
                    l2_muls(j_ - 1, 1)
                for pi_ in range(2):
                    a_, b_, an, bn, sqn, psns, rrn = PRL[pi_]
                    psnn = psns[j_ % 2]
                    psn_t = mk.t[psnn]
                    mk.op("scalar", lambda e, p=psn_t: e.activation(
                        out=nrm[:, 0:CT], in_=p[:, 0:CT], func=AF.Ln,
                        bias=eps2[:, 0:1]),
                        reads=(mk.B(psnn), mk.B("eps2")), writes=(mk.B("psW"),))
                    mk.op("scalar", lambda e, rs=mk.t[rrn]: e.activation(
                        out=rs[:, :], in_=nrm[:, 0:CT], func=AF.Exp, scale=-0.5),
                        reads=(mk.B("psW"),), writes=(mk.B(rrn),))
            l2_muls(3, 0)
            l2_muls(3, 1)
            for (a_, b_, an, bn, ai, bi) in ((fq, fqs, "f_q", "f_qs", 0, 1),
                                             (fk, fks, "f_k", "f_ks", 2, 3)):
                if ci == 0 and li == 0 and an == "f_q":
                    mk.dump_stream("fqn", fq.ap()[:, :], 4 * CT, w, BF16, reads=(mk.B("f_q"),))
                    mk.dump_stream("fqsn", fqs.ap()[:, :], 4 * CT, w, BF16, reads=(mk.B("f_qs"),))
                    mk.dump_stream("frcd", frc.ap()[:, :], 4 * CT, w, BF16, reads=(mk.B("f_rc"),))
                mk.op("vector",
                      [(lambda e, j=j_, t=b_, i_=bi: e.tensor_mul(
                          t[:, j * CT:(j + 1) * CT], t[:, j * CT:(j + 1) * CT],
                          frc[:, i_ * CT:(i_ + 1) * CT])) for j_ in range(4)],
                      reads=(mk.B(bn), mk.B("f_rc")), writes=(mk.B(bn),))
                mk.op("vector",
                      [(lambda e, j=j_, t=a_, i_=ai: e.tensor_mul(
                          t[:, j * CT:(j + 1) * CT], t[:, j * CT:(j + 1) * CT],
                          frc[:, i_ * CT:(i_ + 1) * CT])) for j_ in range(4)],
                      reads=(mk.B(an), mk.B("f_rc")), writes=(mk.B(an),))
                mk.op("vector",
                      [(lambda e, j=j_, t=a_, t2=b_: e.tensor_add(
                          t[:, j * CT:(j + 1) * CT], t[:, j * CT:(j + 1) * CT],
                          t2[:, j * CT:(j + 1) * CT])) for j_ in range(4)],
                      reads=(mk.B(an), mk.B(bn)), writes=(mk.B(an),))
            mk.op("vector", lambda e: e.tensor_copy(kdst[:, :], fk[:, :]),
                  reads=(mk.B("f_k"),), writes=(kdb,))
            if ci == 0 and li == 0:
                mk.dump_stream("fq", fq.ap()[:, :], 4 * CT, w, BF16, reads=(mk.B("f_q"),))
                mk.dump_stream("fk", fk.ap()[:, :], 4 * CT, w, BF16, reads=(mk.B("f_k"),))
                mk.dump_stream("fv", vdst.ap()[:, :] if hasattr(vdst, "ap") else vdst[:, :],
                               4 * CT, w, BF16, reads=(vdb,))

        def attn(w, kprev, vprev, kpb, vpb, kcur, vcur, kcb, vcb, pre_wo=None):
            PAIRS3 = (("psM0", "psM1"), ("psU", "psV"), ("psB", "psW"))
            psu = mk.t["psA"]

            def pslot(hh, ic):
                s = (hh % 2) * 2 + (ic % 2)
                return mk.t[f"f_P{s}"], mk.B(f"f_P{s}")

            def colsb(hh, ic):
                if hh % 2 == 0:
                    return cols, (ic % 2) * 4, mk.B("f_cols")
                return mk.t["f_colsB"], (ic % 2) * 4, mk.B("f_colsB")

            def ptslot(hh, ic):
                s = (hh % 2) * 2 + (ic % 2)
                return fPT[:, s * 640: s * 640 + 640]

            def smm_part(hh, ic, n, hp):
                pspn, pscn = PAIRS3[n % 3]
                psp, psc = mk.t[pspn], mk.t[pscn]
                hrow = 64 * (hh % 2)
                npv = CT - ic * 128
                ncr = ic * 128 + 128
                qsl = fq[hrow:hrow + 64, hp * CT + ic * 128: hp * CT + ic * 128 + 128]
                kp = kprev[hrow:hrow + 64, hp * CT + ic * 128: hp * CT + CT]
                kc_ = kcur[hrow:hrow + 64, hp * CT: hp * CT + ncr]
                if w == 0:
                    mk.op("tensor",
                          [(lambda e, q=qsl, k_=kp, n_=npv, p=psp: e.matmul(
                              p[:, 0:n_], q, k_, start=True, stop=False)),
                           (lambda e, n_=npv, p=psp: e.matmul(
                              p[:, 0:n_], onesr[0:1, 0:128], w0m[0:1, 0:n_],
                              start=False, stop=True))],
                          reads=(kpb, mk.B("f_q"), mk.B("onesr"), mk.B("w0m_t")),
                          writes=(mk.B(pspn),))
                else:
                    mk.op("tensor", lambda e, q=qsl, k_=kp, n_=npv, p=psp: e.matmul(
                        p[:, 0:n_], q, k_, start=True, stop=True),
                        reads=(kpb, mk.B("f_q")), writes=(mk.B(pspn),))
                mk.op("tensor", lambda e, q=qsl, k_=kc_, n_=ncr, p=psc: e.matmul(
                    p[:, 0:n_], q, k_, start=True, stop=True),
                    reads=(kcb, mk.B("f_q")), writes=(mk.B(pscn),))

            def chain_part(hh, ic, n):
                pspn, pscn = PAIRS3[n % 3]
                psp, psc = mk.t[pspn], mk.t[pscn]
                npv = CT - ic * 128
                ncr = ic * 128 + 128
                cl, cb, Bcl = colsb(hh, ic)
                mk.op("vector", lambda e, p=psp: e.tensor_add(
                    p[:, 0:128], p[:, 0:128], mk.t["mtl"][:, :]),
                    reads=(mk.B(pspn), mk.B("mtl")), writes=(mk.B(pspn),))
                mk.op("vector", lambda e, n_=ncr, p=psc: e.tensor_add(
                    p[:, n_ - 128:n_], p[:, n_ - 128:n_], mk.t["mtr"][:, :]),
                    reads=(mk.B(pscn), mk.B("mtr")), writes=(mk.B(pscn),))
                mk.op("vector",
                      [(lambda e, n_=npv, p=psp, c=cb, cc=cl: e.reduce_max(
                          cc[:, c + 1:c + 2], p[:, 0:n_], axis=AX.X)),
                       (lambda e, n_=ncr, p=psc, c=cb, cc=cl: e.reduce_max(
                          cc[:, c + 2:c + 3], p[:, 0:n_], axis=AX.X))],
                      reads=(mk.B(pspn), mk.B(pscn)), writes=(Bcl,))
                mk.op("vector", lambda e, c=cb, cc=cl: e.reduce_max(
                    cc[:, c:c + 1], cc[:, c + 1:c + 3], axis=AX.X, negate=True),
                    reads=(Bcl,), writes=(Bcl,))
                Pb, Bp = pslot(hh, ic)
                mk.op("scalar", lambda e, n_=npv, p=psp, P_=Pb, c=cb, cc=cl: e.activation(
                          out=P_[:, 0:n_], in_=p[:, 0:n_], func=AF.Exp,
                          bias=cc[:, c:c + 1], accum_out=cc[:, c + 1:c + 2]),
                      reads=(mk.B(pspn), Bcl),
                      writes=(Bp, mk.B("f_qin"), Bcl))
                mk.op("scalar", lambda e, n_=ncr, o=npv, p=psc, P_=Pb, c=cb, cc=cl: e.activation(
                          out=P_[:, o:o + n_], in_=p[:, 0:n_], func=AF.Exp,
                          bias=cc[:, c:c + 1], accum_out=cc[:, c + 2:c + 3]),
                      reads=(mk.B(pscn), Bcl),
                      writes=(Bp, mk.B("f_qin"), Bcl))

            def den_part(hh, ic):
                cl, cb, Bcl = colsb(hh, ic)
                Pb, Bp = pslot(hh, ic)
                mk.op("vector", lambda e, c=cb, cc=cl: e.tensor_add(
                    cc[:, c + 1:c + 2], cc[:, c + 1:c + 2], cc[:, c + 2:c + 3]),
                    reads=(Bcl,), writes=(Bcl,))
                mk.op("vector", lambda e, c=cb, cc=cl: e.reciprocal(
                    cc[:, c + 3:c + 4], cc[:, c + 1:c + 2]),
                    reads=(Bcl,), writes=(Bcl,))
                mk.op("vector", lambda e, P_=Pb, c=cb, cc=cl: e.tensor_scalar_mul(
                    P_[:, 0:640], P_[:, 0:640], cc[:, c + 3:c + 4]),
                    reads=(Bp, Bcl), writes=(Bp,))

            def t_part(hh, ic):
                Pb, Bp = pslot(hh, ic)
                PTb = ptslot(hh, ic)
                mk.op("tensor",
                      [(lambda e, j=jcb, sc=(jcb - ic) * 128, P_=Pb: e.transpose(
                          psT[:, j * 128:(j + 1) * 128], P_[:, sc: sc + 128],
                          eye[:, :])) for jcb in range(ic, ic + 5)],
                      reads=(Bp, mk.B("eye_t")), writes=(mk.B("psT"),))
                mk.op("scalar",
                      [(lambda e, j=jcb, d=PTb, sc=(jcb - ic) * 128: e.activation(
                          out=d[:, sc: sc + 128],
                          in_=psT[:, j * 128:(j + 1) * 128], func=AF.Identity))
                       for jcb in range(ic, ic + 5)],
                      reads=(mk.B("psT"),), writes=(mk.B("f_PT"),))

            def pv_part(hh, ic):
                hrow = 64 * (hh % 2)
                PTb = ptslot(hh, ic)
                fns = []
                for jj in range(5):
                    j = ic + jj
                    vsrc = vprev if j < 4 else vcur
                    va = vsrc[:, (j % 4) * CT + hh * 64: (j % 4) * CT + hh * 64 + 64]
                    fns.append(lambda e, v_=va, jx=jj, hr=hrow, i_=ic, d=PTb: e.matmul(
                        psu[hr:hr + 64, i_ * 128:i_ * 128 + 128], v_,
                        d[:, jx * 128: jx * 128 + 128],
                        start=(jx == 0), stop=(jx == 4)))
                mk.op("tensor", fns, reads=(vpb, vcb, mk.B("f_PT")),
                      writes=(mk.B("psA"),))

            seq = [(hp_, ic_) for hp_ in range(4) for ic_ in range(4)]
            for idx in range(len(seq) + 2):
                cur = seq[idx] if idx < len(seq) else None
                prv = seq[idx - 2] if idx >= 2 else None
                for k_ in range(2):
                    if prv is not None:
                        t_part(2 * prv[0] + k_, prv[1])
                    if cur is not None:
                        smm_part(2 * cur[0] + k_, cur[1], idx * 2 + k_, cur[0])
                if cur is not None:
                    chain_part(2 * cur[0], cur[1], idx * 2)
                    chain_part(2 * cur[0] + 1, cur[1], idx * 2 + 1)
                if prv is not None:
                    pv_part(2 * prv[0], prv[1])
                    pv_part(2 * prv[0] + 1, prv[1])
                if cur is not None:
                    den_part(2 * cur[0], cur[1])
                    den_part(2 * cur[0] + 1, cur[1])
                if prv is not None and prv[1] == 3:
                    mk.op("vector", lambda e, t0=prv[0] * CT: e.tensor_copy(
                        fo[:, t0:t0 + CT], psu[:, 0:CT]),
                        reads=(mk.B("psA"),), writes=(mk.B("f_o"),))
            def ev_o(mc, ps, Bps):
                t_, base = xf_ap(mk, mc)
                sl = slice(base + w * CT, base + w * CT + CT)
                mk.op("vector", lambda e, p=ps, ss=sl, tt=t_, m=mc: e.tensor_add(
                    tt[:, ss], tt[:, ss], p[:, 0:CT]),
                    reads=(Bps, xf_buf(mk, mc)), writes=(xf_buf(mk, mc),))
            if ci == 0 and li == 0:
                mk.dump_stream("fo", fo.ap()[:, :], 4 * CT, w, BF16, reads=(mk.B("f_o"),))
            if pre_wo is not None:
                pre_wo()
            proj(mk, lw["o"], "f_o", ev_o, n_mc=FC, kc_n=4, ld_eng="sync",
                 w_names=("w0", "w1", "vm0", "vm1"),
                 ps_names=("psM0", "psM1", "psU", "psV"))

        qkv(0, k0, v0, mk.B("f_k0"), mk.B("f_v0"))
        q0d = mk.dram(f"q0_{ci}{li}", [128, 4 * CT], BF16)
        q0sp = mk.dma("sync", q0d[:, :], fq.ap()[:, :], reads=(mk.B("f_q"),),
                      key=f"q0sp{li}")
        qkv(1, kw[1], vw[1], mk.B("f_kb"), mk.B("f_vb"))
        attn(1, k0, v0, mk.B("f_k0"), mk.B("f_v0"),
             kw[1], vw[1], mk.B("f_kb"), mk.B("f_vb"),
             pre_wo=lambda: build_qin(2))
        qkv(2, kw[0], vw[0], mk.B("f_ka"), mk.B("f_va"), skip_qin=True)
        attn(2, kw[1], vw[1], mk.B("f_kb"), mk.B("f_vb"),
             kw[0], vw[0], mk.B("f_ka"), mk.B("f_va"),
             pre_wo=lambda: build_qin(3))
        qkv(3, kw[1], vw[1], mk.B("f_kb"), mk.B("f_vb"), skip_qin=True)
        attn(3, kw[0], vw[0], mk.B("f_ka"), mk.B("f_va"),
             kw[1], vw[1], mk.B("f_kb"), mk.B("f_vb"))
        bb16 = mk.t["Bb"].ap().bitcast(BF16)
        cc_kv, in_kv, out_kv, b_kv = send_pair(
            mk, f"kv{ci}l{li}", bb16[:, 4096:8192], (mk.B("f_kb"), mk.B("f_vb")),
            4096, BF16)
        recv_pair(mk, f"kv{ci}l{li}", cc_kv, in_kv, out_kv, bb16[:, 0:4096],
                  (mk.B("f_ka"), mk.B("f_va")), b_kv)
        mk.dma("sync", fq.ap()[:, :], q0d[:, :], writes=(mk.B("f_q"),),
               deps=(q0sp,), key=f"q0ld{li}")
        attn(0, kw[0], vw[0], mk.B("f_ka"), mk.B("f_va"),
             k0, v0, mk.B("f_k0"), mk.B("f_v0"))
        dump_xf(f"ffw{li}")
        mk.barrier()
        mk.dma("sync", mk.t[f"b1b{li}"][:, :], W[f"b1{li}"][:, :],
               writes=(mk.B(f"b1b{li}"),), key=f"bb{li}")

        cc_s2, in_s2, out_s2, b_s2 = send_shift(f"s2c{ci}l{li}")
        recv_pair(mk, f"s2c{ci}l{li}", cc_s2, in_s2, out_s2, shb2[:, 0:4],
                  (mk.B("shb2"),), b_s2)
        mk.op("vector", lambda e: e.tensor_scalar_mul(shb2[:, 4:8], shb2[:, 0:4],
                                                      c0_t[:, 0:1]),
              reads=(mk.B("shb2"), mk.B("c0_t")), writes=(mk.B("shb2"),))
        mk.op("vector",
              [(lambda e, k=fc: e.tensor_copy(
                  sb3[:, (k - 4) * 3:(k - 4) * 3 + 3],
                  t47[:, (k % 4) * T + CT - 1: (k % 4) * T + 3 * CT: CT]))
               for fc in range(4, FC)],
              reads=(mk.B("xf47"),), writes=(mk.B("sb3"),))

        hg1, hg2 = mk.t["f_hg1"], mk.t["f_hg2"]
        def hg_sl(kc):
            if kc < 16:
                return hg1[:, kc * CT:(kc + 1) * CT], mk.B("Bb")
            return hg2[:, (kc - 16) * CT:(kc - 16 + 1) * CT], mk.B("f_hg2")

        xalt_h, xalt_x, sq2 = mk.t["xalt_h"], mk.t["xalt_x"], mk.t["ln_sq2"]

        def alt_ap(fc):
            if fc < 5:
                return xalt_h[:, fc * CT:(fc + 1) * CT], mk.B("xalt_h")
            return xalt_x[:, (fc - 5) * CT:(fc - 4) * CT], mk.B("xalt_x")

        def sq2_sl(fc):
            return sq2[:, (fc % 2) * CT:((fc % 2) + 1) * CT], mk.B("ln_sq2")

        def emit_ln(sl_i, alt):
            def seg(fc, s_=sl_i):
                t_, base = xf_ap(mk, fc)
                o = s_ * CT
                if fc < 4:
                    return [(0, CT, t_[:, base + o: base + o + CT])]
                return [(1, CT - 1, t_[:, base + o: base + o + CT - 1])]
            if sl_i == 0:
                patches = [(fc, shb2[:, fc: fc + 1], mk.B("shb2"))
                           for fc in range(4, FC)]
            else:
                patches = [(fc, sb3[:, (fc - 4) * 3 + sl_i - 1: (fc - 4) * 3 + sl_i],
                            mk.B("sb3")) for fc in range(4, FC)]
            kw = {"sq_sl": sq2_sl}
            if alt:
                kw = {"xb_sl": alt_ap, "dst_sl": alt_ap, "sq_sl": sq2_sl}
            layernorm(mk, None, sl_i, "xt", src_seg=seg,
                      src_bufs=(mk.B("xf03"), mk.B("xf47")),
                      patches=patches, affine=(lgp, lbp, (mk.B(f"lgb{li}"), mk.B(f"lbb{li}"))),
                      u_name="ln_u2", **kw)

        slabs = (1, 2, 3, 0)
        ALTF = (True, False, True, False)
        emit_ln(slabs[0], ALTF[0])
        for sidx, sl_i in enumerate(slabs):
            t0 = sl_i * CT
            if sidx + 1 < 4:
                emit_ln(slabs[sidx + 1], ALTF[sidx + 1])
            if ALTF[sidx]:
                rhs_fn = lambda kc: alt_ap(kc)[0]
                rhs_bufs = (mk.B("xalt_h"), mk.B("xalt_x"))
            else:
                rhs_fn = lambda kc: mk.t["xt"][:, kc * CT:(kc + 1) * CT]
                rhs_bufs = (mk.B("xt"),)
            for i_ in range(22):
                pa, pb = (("psM0", "psM1") if i_ % 2 == 0 else ("psU", "psV"))
                wna, wnb = ("w0", "w1") if i_ % 2 == 0 else ("vm0", "vm1")
                for wn, psn, mc in ((wna, pa, i_), (wnb, pb, 22 + i_)):
                    wb = mk.t[wn]
                    mk.dma("sync", wb[:, :], lw["1"][mc, :, :], writes=(mk.B(wn),), key=wn)
                    ps = mk.t[psn]
                    mk.op("tensor",
                          [(lambda e, k_=kc, p=ps, w_=wb, rf=rhs_fn: e.matmul(
                              p[:, 0:CT], w_[:, k_ * 128:(k_ + 1) * 128],
                              rf(k_),
                              start=(k_ == 0), stop=(k_ == FC - 1))) for kc in range(FC)],
                          reads=(mk.B(wn),) + rhs_bufs, writes=(mk.B(psn),))
                g = gel[i_ % 2]
                gn = "xp"
                mk.op("scalar", lambda e, i__=i_, gg=g, bp=b1p, p=mk.t[pb]: e.activation(
                    out=gg[:, :], in_=p[:, 0:CT], func=AF.Gelu,
                    bias=bp[:, 22 + i__: 23 + i__]),
                    reads=(mk.B(pb), mk.B(f"b1b{li}")), writes=(mk.B(gn),))
                hga, hgb = hg_sl(i_)
                mk.op("vector", lambda e, i__=i_, gg=g, d=hga, bp=b1p, p=mk.t[pa]:
                      e.scalar_tensor_tensor(
                          d, p[:, 0:CT], bp[:, i__:i__ + 1], gg[:, :], ALU.add, ALU.mult),
                      reads=(mk.B(pa), mk.B(gn), mk.B(f"b1b{li}")), writes=(hgb,))
            pss = ["psA", "psB", "psM0", "psM1"]
            for half in range(2):
                for kc in range(22):
                    wn = ("w0", "w1", "vm0", "vm1")[kc % 4]
                    wb = mk.t[wn]
                    mk.dma("sync", wb[:, 0:CT], lw["2"][kc, :, half * CT:(half + 1) * CT],
                           writes=(mk.B(wn),), key=wn)
                    hga, hgb = hg_sl(kc)
                    mk.op("tensor",
                          [(lambda e, k_=kc, v_=vi, w_=wb, hh=hga: e.matmul(
                              mk.t[pss[v_]][:, 0:CT], w_[:, v_ * 128:(v_ + 1) * 128], hh,
                              start=(k_ == 0), stop=(k_ == 21))) for vi in range(4)],
                          reads=(mk.B(wn), hgb),
                          writes=tuple(mk.B(p) for p in pss))
                for vi in range(4):
                    dm = half * 4 + vi
                    t_, base = xf_ap(mk, dm)
                    ps = mk.t[pss[vi]]
                    sl = slice(base + t0, base + t0 + CT)
                    mk.op("vector",
                          lambda e, p=ps, ss=sl, tt=t_, d_=dm, bp=b2p: e.scalar_tensor_tensor(
                              tt[:, ss], p[:, 0:CT], bp[:, d_:d_ + 1], tt[:, ss],
                              ALU.add, ALU.add),
                          reads=(mk.B(pss[vi]), xf_buf(mk, dm), mk.B(f"b2b{li}")),
                          writes=(xf_buf(mk, dm),))
        dump_xf(f"ffs{li}")
        mk.barrier()

    fns = []
    for fc in range(FC):
        t_, base = xf_ap(mk, fc)
        fns.append(lambda e, tt=t_, b_=base, k=fc: e.tensor_add(
            x_t[:, k * T:(k + 1) * T], x_t[:, k * T:(k + 1) * T], tt[:, b_: b_ + T]))
    mk.op("vector", fns, reads=(mk.B("xf03"), mk.B("xf47"), Bx), writes=(Bx,))

def build(dbg_names=(), parts=("a", "f", "b")):
    import contextlib
    mk = MK(dbg_names)
    stack = contextlib.ExitStack()
    mk._exit = stack
    nc = mk.nc

    W = {}
    x_in = mk.din("x", [128, FC * T])
    for n in ("wa_in", "wa_al", "wa_be", "wa_v", "wa_out", "wb_q", "wb_k"):
        W[n] = mk.din(n, [FC, 128, D])
    W["wb_v"] = mk.din("wb_v", [FC, 128, D])
    for n, key in (("bal", "ba_al"), ("bbe", "ba_be"), ("bvv", "ba_v"),
                   ("bbq_t", "bb_q"), ("bbk_t", "bb_k")):
        W[key] = mk.din(key, [128, FC])
    c0_in = mk.din("c0", [128, 1])
    W["gk"] = mk.din("gk", [128, 4 * NCH])
    for l in range(2):
        W[f"wf_qkv{l}"] = mk.din(f"wf_qkv{l}", [16, 128, 1024])
        W[f"wf_vm{l}"] = mk.din(f"wf_vm{l}", [8, 128, 512])
        W[f"wf_o{l}"] = mk.din(f"wf_o{l}", [8, 128, 512])
        W[f"wf_1{l}"] = mk.din(f"wf_1{l}", [44, 128, 1024])
        W[f"wf_2{l}"] = mk.din(f"wf_2{l}", [22, 128, 1024])
        W[f"qsk{l}"] = mk.din(f"qsk{l}", [128, 4])
        W[f"lg{l}"] = mk.din(f"lg{l}", [128, FC])
        W[f"lb{l}"] = mk.din(f"lb{l}", [128, FC])
        W[f"b1{l}"] = mk.din(f"b1{l}", [128, 44])
        W[f"b2{l}"] = mk.din(f"b2{l}", [128, FC])
    W["rot"] = mk.din("rot", [128, 4, T], BF16)
    W["w0m"] = mk.din("w0m", [1, CT], BF16)
    W["gq"] = mk.din("gq", [128, T], BF16)
    gc_in = mk.din("gc", [128, 2 * CT], BF16)
    dm_in = mk.din("dmask", [128, 4 * CT], BF16)
    out_x = mk.dout("out", [128, FC * T])

    mk.t["x1_in"] = View(mk.dram("x1_in", [128, FC]))
    mk.t["x1_out"] = View(mk.dram("x1_out", [256, FC]))
    mk.t["x8_in"] = View(mk.dram("x8_in", [128, FC * D], BF16))
    mk.t["x8_out"] = View(mk.dram("x8_out", [256, FC * D], BF16))
    mk._reg("x8_sel", mk.dram("x8_sel", [128, FC * D], BF16), None)
    for l in range(2):
        mk._reg(f"wst_qkv{l}", mk.dram(f"wst_qkv{l}", [16, 128, 1024], BF16), None)
        mk._reg(f"wst_vm{l}", mk.dram(f"wst_vm{l}", [8, 128, 512], BF16), None)
        mk._reg(f"wst_o{l}", mk.dram(f"wst_o{l}", [8, 128, 512], BF16), None)
        mk._reg(f"wst_1{l}", mk.dram(f"wst_1{l}", [44, 128, 1024], BF16), None)
        mk._reg(f"wst_2{l}", mk.dram(f"wst_2{l}", [22, 128, 1024], BF16), None)
    for _n in ("ain", "av", "abe", "aal", "aout", "bq", "bk"):
        mk._reg(f"wst_{_n}", mk.dram(f"wst_{_n}", [8, 128, 1024], BF16), None)

    mk.sb("x_fm", [128, FC * T], F32)
    mk.sb("xt", [128, FC * CT], BF16)
    mk.sb("xp", [128, FC * CT], BF16)
    mk.sb("Vb", [128, FC * CT], F32)
    mk.sb("Bb", [128, FC * CT], F32)
    mk.sb("hch", [128, FC * CT], F32)
    mk.sb("hfull", [128, FC * T], BF16)
    mk.sb("pfull", [128, FC * T], BF16)
    mk.sb("w0", [128, FC * 128], BF16)
    mk.sb("w1", [128, FC * 128], BF16)
    for n in ("hcar", "pcar", "hin", "hinm"):
        mk.sb(n, [128, FC], F32)
    mk.sb("c0_t", [128, 1], F32)
    mk.sb("zb", [128, CT], F32)
    mk.sb("ones_bf", [128, 128], BF16)
    mk.sb("eps", [128, 1], F32)
    for n in ("bal", "bbe", "bvv", "bbq_t", "bbk_t"):
        mk.sb(n, [128, FC], F32)
    mk.view("ln_xb", "xt", BF16, 0, FC * CT, share="xt")
    mk.view("ln_sq", "xp", BF16, 0, FC * CT, share="xp")
    mk.view("ln_u", "Bb", F32, 0, FC * CT, share="Bb")
    mk.sb("ln_m", [128, CT], F32)
    mk.sb("ln_msq", [128, CT], F32)
    mk.view("ln_r", "ln_msq", F32, 0, CT, share="ln_msq")
    mk.view("tmpS", "Vb", F32, 0, FC * CT, share="Vb")
    mk.view("q_fm", "Vb", BF16, 0, FC * CT)
    mk.view("ptr", "Vb", BF16, FC * CT, 4 * CT)
    mk.view("k_fm", "Bb", BF16, 0, FC * CT, share="Bb")
    mk.view("ktm", "Bb", BF16, FC * CT, 4 * D, share="Bb")
    mk.view("vtm", "hch", BF16, 0, 4 * D)
    mk.view("qgs", "xp", BF16, 0, FC * CT, share="xp")
    mk.view("qgc", "hfull", BF16, 0, FC * 2 * CT)
    mk.view("wvm", "hfull", BF16, FC * 2 * CT, FC * D)
    mk.view("A_sb", "pfull", BF16, 0, FC * D)
    mk.view("gq_t", "pfull", BF16, FC * D, T)
    mk.view("dmask_t", "pfull", BF16, FC * D + T, 4 * CT)
    mk.view("gc_t", "pfull", BF16, FC * D + T + 4 * CT, 2 * CT)
    mk.view("gk_t", "pfull", F32, (FC * D + T + 6 * CT) // 2, 4 * NCH)
    mk.sb("eye_t", [128, 128], BF16)
    mk.view("Ain0", "hch", BF16, 4 * D, D)
    mk.view("Ain1", "hch", BF16, 5 * D, D)
    mk.view("xf03", "hfull", F32, 0, 4 * T)
    mk.view("xf47", "pfull", F32, 0, 4 * T)
    mk.view("f_qin", "xt", BF16, 0, FC * CT, share="xt")
    mk.view("f_P", "xt", BF16, 0, 4 * 640, share="xt")
    for _s in range(4):
        mk.view(f"f_P{_s}", "xt", BF16, _s * 640, 640)
    mk.view("f_sqv", "xt", BF16, 4 * 640, CT, share="xt")
    mk.view("f_sqv2", "xt", BF16, 4 * 640 + CT, CT, share="xt")
    mk.view("f_q", "Vb", BF16, 0, 4 * CT)
    mk.view("f_qs", "Vb", BF16, 4 * CT, 4 * CT)
    mk.view("f_k", "Vb", BF16, 8 * CT, 4 * CT)
    mk.view("f_ks", "Vb", BF16, 12 * CT, 4 * CT)
    mk.view("ln_u2", "Vb", F32, 0, FC * CT)
    mk.view("f_ka", "Bb", BF16, 0, 4 * CT)
    mk.view("f_va", "Bb", BF16, 4 * CT, 4 * CT)
    mk.view("f_kb", "Bb", BF16, 8 * CT, 4 * CT)
    mk.view("f_vb", "Bb", BF16, 12 * CT, 4 * CT)
    mk.view("f_hg1", "Bb", BF16, 0, FC * 2 * CT, share="Bb")
    mk.view("f_k0", "hch", BF16, 0, 4 * CT)
    mk.view("f_v0", "hch", BF16, 4 * CT, 4 * CT)
    mk.view("f_PT", "hch", BF16, 8 * CT, 4 * 640)
    mk.view("f_hg2", "hch", BF16, 0, 6 * CT)
    mk.view("f_o", "xp", BF16, 0, 4 * CT, share="xp")
    mk.view("f_rc", "xp", BF16, 4 * CT, 4 * CT, share="xp")
    mk.view("f_rr0", "hch", BF16, 6656, CT)
    mk.view("xalt_h", "hch", BF16, 4096, 5 * CT)
    mk.view("xalt_x", "xp", BF16, 1024, 3 * CT)
    mk.view("ln_sq2", "xp", BF16, 2560, 2 * CT)
    mk.view("f_rr1", "hch", BF16, 6656 + CT, CT)
    mk.sb("f_cols", [128, 16], F32)
    mk.view("f_colsB", "f_cols", F32, 8, 8)
    mk.sb("shb1", [128, 8], F32)
    mk.sb("shb2", [128, 8], F32)
    mk.sb("sb3", [128, 12], F32)
    mk.sb("vm0", [128, 1024], BF16)
    mk.sb("vm1", [128, 1024], BF16)
    mk.view("gel0", "xp", BF16, 0, CT, share="xp")
    mk.view("gel1", "xp", BF16, CT, CT, share="xp")
    mk.view("hdones", "zb", BF16, CT, 128)
    mk.view("mtl", "zb", BF16, CT + 128, 128)
    mk.view("mtr", "zb", BF16, CT + 256, 128)
    mk.sb("eps2", [128, 1], F32)
    mk.sb("onesr", [1, 128], BF16)
    mk.view("w0m_t", "zb", BF16, 0, CT)
    for l in range(2):
        mk.sb(f"qskb{l}", [128, 4], F32)
        mk.sb(f"lgb{l}", [128, FC], F32)
        mk.sb(f"lbb{l}", [128, FC], F32)
        mk.view(f"b1b{l}", "hch", F32, 3840 + 44 * l, 44)
        mk.sb(f"b2b{l}", [128, FC], F32)

    mk.ps("psA"); mk.ps("psB"); mk.ps("psM0"); mk.ps("psM1")
    mk.ps("psU"); mk.ps("psV"); mk.ps("psW")
    mk.ps("psT", (128, 1024), BF16)

    mk.dma("sync", mk.t["x_fm"][:, :], x_in[:, :], writes=(mk.B("x_fm"),), key="init")
    for dst, src in (("bal", W["ba_al"]), ("bbe", W["ba_be"]), ("bvv", W["ba_v"]),
                     ("bbq_t", W["bb_q"]), ("bbk_t", W["bb_k"]),
                     ("c0_t", c0_in)):
        mk.dma("sync", mk.t[dst][:, :], src[:, :], writes=(mk.B(dst),), key="init")
    W["gc"] = gc_in
    W["dm"] = dm_in
    mk.op("vector", lambda e: e.memset(mk.t["ones_bf"][:, :], 1.0), writes=(mk.B("ones_bf"),))
    mk.op("vector", lambda e: e.memset(mk.t["eps"][:, :], 1e-5), writes=(mk.B("eps"),))
    mk.op("vector", lambda e: e.memset(mk.t["eps2"][:, :], 1e-24), writes=(mk.B("eps2"),))
    mk.op("vector", lambda e: e.memset(mk.t["zb"][:, :], 0.0), writes=(mk.B("zb"),))
    mk.op("vector", lambda e: e.memset(mk.t["onesr"][:, :], 1.0), writes=(mk.B("onesr"),))
    eyev = mk.nc.inline_tensor(np.eye(128, dtype=ml_dtypes.bfloat16), "eye128").ap()
    mk.dma("sync", mk.t["eye_t"][:, :], eyev[:, :], writes=(mk.B("eye_t"),), key="init")
    for l in range(2):
        for dst, src in ((f"qskb{l}", W[f"qsk{l}"]), (f"lgb{l}", W[f"lg{l}"]),
                         (f"lbb{l}", W[f"lb{l}"]), (f"b2b{l}", W[f"b2{l}"])):
            mk.dma("sync", mk.t[dst][:, :], src[:, :], writes=(mk.B(dst),), key="init")

    if "f" in parts:
        stage_weights(mk, W)
    if "a" in parts:
        build_attn_a(mk, W)
        mk.barrier()
    if "f" in parts:
        build_ffn(mk, W, 0)
        mk.barrier()
    if "b" in parts:
        build_attn_b(mk, W)
        mk.barrier()
    if "f" in parts:
        build_ffn(mk, W, 1)
        mk.barrier()

    mk.dma("sync", out_x[:, :], mk.t["x_fm"][:, :], reads=(mk.B("x_fm"),), key="outx")
    final_deps = [op for op in mk.f.ops if op.kind in ("dma", "switch", "cc")]
    mk.f.emit("sync", lambda e: e.nop(), deps=tuple(final_deps), kind="op")

    mk.f.lower(nc, stack)
    stack.close()
    return mk


def gamma_tiles():
    lg = np.log(GAMMA)
    j = np.arange(128)
    gk = np.zeros((128, 4 * NCH), np.float32)
    for c in range(NCH):
        for jc in range(4):
            gk[:, c * 4 + jc] = np.exp(-(c * CT + jc * 128 + j) * lg)
    i = np.arange(T)
    gq_t = np.broadcast_to(np.exp(i * lg), (128, T)).astype(ml_dtypes.bfloat16)
    ic = np.arange(2 * CT)
    gc_t = np.broadcast_to(np.exp((ic + T) * lg), (128, 2 * CT)).astype(ml_dtypes.bfloat16)
    dm = np.zeros((128, 4 * CT), np.float64)
    for jc in range(4):
        jj = jc * 128 + j[:, None]
        ii = np.arange(CT)[None, :]
        dm[:, jc * CT:(jc + 1) * CT] = np.where(ii >= jj, np.exp((ii - jj) * lg), 0.0)
    return gk, np.ascontiguousarray(gq_t), np.ascontiguousarray(gc_t), \
        dm.astype(ml_dtypes.bfloat16)


def host_inputs(inp, core):
    b, half = core // 2, core % 2
    x = np.asarray(inp["L0_attn_a_x"])
    tsl = slice(half * T, (half + 1) * T)
    xs = np.ascontiguousarray(x[b, tsl, :].T)
    x_fm = np.ascontiguousarray(xs.reshape(FC, 128, T).transpose(1, 0, 2).reshape(128, FC * T))
    gk, gq_t, gc_t, dm = gamma_tiles()
    d = {
        "x": x_fm.astype(np.float32),
        "wa_in": pack_w(np.asarray(inp["L0_attn_a_W_in"]).T),
        "wa_al": pack_w(np.asarray(inp["L0_attn_a_W_alpha"]).T),
        "wa_be": pack_w(np.asarray(inp["L0_attn_a_W_beta"]).T),
        "wa_v": pack_w(np.asarray(inp["L0_attn_a_W_v"]).T),
        "wa_out": pack_w(np.asarray(inp["L0_attn_a_W_out"]).T),
        "ba_al": pack_bias(inp["L0_attn_a_b_alpha"]),
        "ba_be": pack_bias(inp["L0_attn_a_b_beta"]),
        "ba_v": pack_bias(inp["L0_attn_a_b_v"]),
        "wb_q": pack_w(np.asarray(inp["L0_attn_b_Wq"]).T),
        "wb_k": pack_w(np.asarray(inp["L0_attn_b_Wk"]).T),
        "wb_v": pack_w_mov(np.asarray(inp["L0_attn_b_Wv"]).T),
        "bb_q": pack_bias(inp["L0_attn_b_bq"]),
        "bb_k": pack_bias(inp["L0_attn_b_bk"]),
        "c0": np.full((128, 1), float(half), np.float32),
        "gk": gk,
        "gq": gq_t,
        "gc": gc_t if half == 1 else np.zeros_like(gc_t),
        "dmask": dm,
        "rot": rot_tiles(half),
        "w0m": (np.full((1, CT), -1e30) if half == 0 else np.zeros((1, CT))
                ).astype(ml_dtypes.bfloat16),
    }
    perm = np.array([h * 64 + (dd + 32) % 64 for h in range(8) for dd in range(64)])
    perm64 = np.array([(dd + 32) % 64 for dd in range(64)])
    wqkv = np.asarray(inp["L0_ffn_wqkv"], np.float32)
    wo = np.asarray(inp["L0_ffn_wo"], np.float32)
    w1 = np.asarray(inp["L0_ffn_w1"], np.float32)
    w2 = np.asarray(inp["L0_ffn_w2"], np.float32)
    b1 = np.asarray(inp["L0_ffn_b1"], np.float32)
    b2 = np.asarray(inp["L0_ffn_b2"], np.float32)
    qs = np.asarray(inp["L0_ffn_q_scale"], np.float32)
    ks = np.asarray(inp["L0_ffn_k_scale"], np.float32)
    lg = np.asarray(inp["L0_ffn_ln_g"], np.float32)
    lb = np.asarray(inp["L0_ffn_ln_b"], np.float32)
    for l in range(2):
        wq = wqkv[l][:, 0:512]
        wk = wqkv[l][:, 512:1024]
        wv = wqkv[l][:, 1024:1536]
        d[f"wf_qkv{l}"] = pack_w(np.concatenate(
            [wq, wq[:, perm], wk, wk[:, perm]], axis=1))
        d[f"wf_vm{l}"] = pack_w_mov(wv)
        d[f"wf_o{l}"] = pack_w(wo[l])
        a_, g_ = w1[l][:, 0:2730], w1[l][:, 2730:5460]
        w1p = np.concatenate([np.pad(a_, ((0, 0), (0, 86))),
                              np.pad(g_, ((0, 0), (0, 86)))], axis=1)
        d[f"wf_1{l}"] = pack_w(w1p)
        d[f"wf_2{l}"] = pack_w_mov(np.pad(w2[l], ((0, 86), (0, 0))))
        b1a = np.pad(b1[l][0:2730], (0, 86))
        b1g = np.pad(b1[l][2730:5460], (0, 86))
        d[f"b1{l}"] = pack_bias(np.concatenate([b1a, b1g]))
        d[f"b2{l}"] = pack_bias(b2[l])
        d[f"lg{l}"] = pack_bias(lg[l])
        d[f"lb{l}"] = pack_bias(lb[l])
        d[f"qsk{l}"] = np.stack([np.tile(qs[l], 2), np.tile(qs[l][perm64], 2),
                                 np.tile(ks[l], 2), np.tile(ks[l][perm64], 2)],
                                axis=1).astype(np.float32)
    return d


def rot_tiles(half):
    DIM_HEAD, WINDOW, QK = 64, 512, 8.0
    d128 = np.arange(128)
    d64 = d128 % 64
    h32 = d64 % 32
    inv = 1.0 / 10000.0 ** (2.0 * h32 / DIM_HEAD)
    t = np.arange(half * T, (half + 1) * T, dtype=np.float64)
    th = t[None, :] * inv[:, None]
    base = (2.0 * h32 + 0.4 * DIM_HEAD) / (1.4 * DIM_HEAD)
    power = (t - 2048.0) / (WINDOW // 2)
    sc = base[:, None] ** power[None, :]
    sign = np.where(d64 < 32, -1.0, 1.0)[:, None]
    Aq = np.cos(th) * sc * QK
    Bq = sign * np.sin(th) * sc * QK
    Ak = np.cos(th) / sc
    Bk = sign * np.sin(th) / sc
    out = np.stack([Aq, Bq, Ak, Bk], axis=1)
    return np.ascontiguousarray(out).astype(ml_dtypes.bfloat16)


def host_output(res_list):
    out = np.zeros((4, 4096, 1024), np.float32)
    for core in range(8):
        b, half = core // 2, core % 2
        o = np.asarray(res_list[core]["out"])
        o = o.reshape(128, FC, T).transpose(1, 0, 2).reshape(1024, T)
        out[b, half * T:(half + 1) * T, :] = o.T
    return out


_BUILT = None


def kernel(**inputs):
    global _BUILT
    from concourse.bass_utils import run_bass_kernel_spmd
    if _BUILT is None:
        _BUILT = build()
    mkk = _BUILT
    ins = [host_inputs(inputs, c) for c in range(8)]
    res = run_bass_kernel_spmd(mkk.nc, ins, core_ids=list(range(8)))
    return host_output(res.results)
